# version 18
# baseline (speedup 1.0000x reference)
import numpy as np
import contextlib
import concourse.bass as bass
import concourse.mybir as mybir
from concourse.bass_utils import run_bass_kernel_spmd

F32 = mybir.dt.float32
BF16 = mybir.dt.bfloat16
I32 = mybir.dt.int32
OP = mybir.AluOpType
AF = mybir.ActivationFunctionType
AX = mybir.AxisListType.X

S = 2048
D = 1024
NH, HD = 16, 64
DH = 4096
KD = 8
KH = 32
TG = 512
NTG = 4
EPS = 1.0e-5
ENGS = ("sync", "vector", "scalar", "tensor", "gpsimd")
NENG = len(ENGS)
DINV = 1.0 / D


_DTSZ = {F32: 4, BF16: 2, I32: 4}


def _regions(aps):
    regs = []
    for a in aps:
        if a is None or not hasattr(a, "tensor"):
            continue
        es = _DTSZ.get(a.dtype, 4)
        pairs = list(a.ap)
        pstride, pnum = pairs[0]
        row_b = int(pstride) * es
        off_b = int(a.offset) * es
        p0 = off_b // row_b if row_b > 0 else 0
        c0 = off_b - p0 * row_b
        span = 0
        for stn in pairs[1:]:
            span += abs(int(stn[0])) * (int(stn[1]) - 1)
        regs.append((a.tensor.name, p0, p0 + int(pnum), c0,
                     c0 + span * es + es))
    return regs


def _clash(regs1, regs2):
    for r1 in regs1:
        for r2 in regs2:
            if (r1[0] == r2[0] and r1[1] < r2[2] and r2[1] < r1[2]
                    and r1[3] < r2[4] and r2[3] < r1[4]):
                return True
    return False


class Wrap:
    def __init__(self, eng):
        self.e = eng
        self.written = []
        self.readset = []
        self.last = None

    def begin_stage(self):
        self.written = []
        self.readset = []
        self.last = None

    def do(self, outs, ins, fn, no_drain=False):
        oreg = _regions(outs)
        ireg = _regions(ins)
        if not no_drain and (_clash(oreg, self.written)
                             or _clash(oreg, self.readset)
                             or _clash(ireg, self.written)):
            self.e.drain()
            self.written = []
            self.readset = []
        inst = fn(self.e)
        self.written.extend(oreg)
        self.readset.extend(ireg)
        self.last = inst
        return inst

    def wait(self, sem, val):
        self.e.wait_ge(sem, val)
        self.last = None

    def raw(self, fn):
        inst = fn(self.e)
        self.last = None
        return inst


def build_program(debug=False):
    nc = bass.Bass("TRN2", target_bir_lowering=False)

    def din(name, shape, dt=F32):
        return nc.dram_tensor(name, shape, dt, kind="ExternalInput").ap()

    xT = din("xT", [D, S])
    wq_d = din("wq", [D, D])
    wk_d = din("wk", [D, D])
    wv_d = din("wv", [D, D])
    wo_d = din("wo", [D, D])
    wbeta_d = din("wbeta", [D, NH])
    bbeta_d = din("bbeta", [NH, 1])
    bo_d = din("bo128", [128, KD])
    grow_d = din("g_row", [1, D])
    lrow_d = din("lnb_row", [1, D])
    g128_d = din("g128", [128, KD])
    lnb128_d = din("lnb128", [128, KD])
    convw_d = din("convw", [128, KH * 3])
    convb_d = din("convb", [128, KH])
    w1T_d = din("w1T", [D, DH])
    w2T_d = din("w2T", [DH, D])
    e16f_d = din("e16f", [NH, KD * 128])
    e16b_d = din("e16b", [NH, KD * 128], BF16)
    eye_d = din("eye128", [128, 128])
    onesrow_d = din("onesrow", [1, 128])

    outT = nc.dram_tensor("outT", [D, S], F32, kind="ExternalOutput").ap()
    w1t_dram = nc.dram_tensor("w1t_scr", [D, DH], BF16, kind="Internal").ap()
    w2t_dram = nc.dram_tensor("w2t_scr", [DH, D], BF16, kind="Internal").ap()

    dbg = {}
    if debug:
        for nm, shape in [("dbg_xinq", [D, S]), ("dbg_oT", [D, S]),
                          ("dbg_y", [D, S]), ("dbg_x2", [D, S]),
                          ("dbg_xqv", [D, TG]), ("dbg_hs", [DH, TG]),
                          ("dbg_out2", [D, TG])]:
            dbg[nm] = nc.dram_tensor(nm, shape, BF16, kind="ExternalOutput").ap()
        dbg["dbg_stt"] = nc.dram_tensor("dbg_stt", [128, 32], F32,
                                        kind="ExternalOutput").ap()
        dbg["dbg_ps"] = nc.dram_tensor("dbg_ps", [128, 24], F32,
                                       kind="ExternalOutput").ap()
        dbg["dbg_q"] = nc.dram_tensor("dbg_q", [128, 2048], F32,
                                      kind="ExternalOutput").ap()
        dbg["dbg_rows"] = nc.dram_tensor("dbg_rows", [1, 10 * TG], F32,
                                         kind="ExternalOutput").ap()

    ctx = contextlib.ExitStack()

    def sbuf(name, shape, dt):
        return ctx.enter_context(nc.sbuf_tensor(name, shape, dt))

    x2bf_t = sbuf("x2bf", [128, KD * S], BF16)
    x2bf = x2bf_t[:].rearrange("p (k t) -> p k t", k=KD)

    arB_t = sbuf("arB", [128, KD * S], F32)
    arB_bf = arB_t[:].bitcast(BF16)
    xinq = arB_bf[:, 0:KD * S].rearrange("p (k t) -> p k t", k=KD)
    oTv = arB_bf[:, KD * S:2 * KD * S].rearrange("p (c t) -> p c t", c=KD)
    hs_q = arB_bf[:, 0:KH * TG].rearrange("p (c t) -> p c t", c=KH)
    w1g = [arB_bf[:, KD * S + i * 8192:KD * S + (i + 1) * 8192]
           .rearrange("p (k c) -> p k c", k=KD) for i in range(2)]

    arC_t = sbuf("arC", [128, 8192], F32)
    arC = arC_t[:]
    beta_b = arC[:, 0:S]
    vsqs_b = arC[:, S:2 * S]
    u_sc = arC[:, 2 * S:3 * S]
    g_sc = arC[:, 3 * S:4 * S]
    Yv = arC_t[:].bitcast(BF16)[:, 0:KD * S].rearrange("p (k t) -> p k t", k=KD)
    t1buf = arC[:, 0:8 * TG].rearrange("p (c t) -> p c t", c=8)
    dbuf = arC[:, 0:8 * TG].bitcast(I32).rearrange("p (c t) -> p c t", c=8)
    u0buf = arC[:, 8 * TG:16 * TG].rearrange("p (c t) -> p c t", c=8)
    xfbuf = arC[:, 8 * TG:16 * TG].rearrange("p (k t) -> p k t", k=KD)
    wpT = [arC[:, 4096 + i * 1024:4096 + (i + 1) * 1024].bitcast(BF16)
           for i in range(2)]

    arD_t = sbuf("arD", [128, 4096], F32)
    arD_bf = arD_t[:].bitcast(BF16)
    wk_bf = arD_bf.rearrange("p (k j) -> p k j", k=KD)
    wo_bf = wk_bf
    ysq = arD_bf[:, 0:KD * TG].rearrange("p (k t) -> p k t", k=KD)
    x2sq = ysq
    w2chunk = arD_bf.rearrange("p (k j) -> p k j", k=8)

    arE_t = sbuf("arE", [128, 4736], F32)
    mmx = sbuf("mmx", [128, 1024], BF16)
    mmn = sbuf("mmn", [128, 1024], F32)
    arE = arE_t[:]
    xst = [arE[:, 0:2048], arE[:, 2048:4096]]
    pbuf = [arE[:, 1024:1536], arE[:, 1536:2048],
            arE[:, 3072:3584], arE[:, 3584:4096]]
    betaT = arE[0:NH, 0:S]
    vsum_bf = arE[0:NH, S:S + S // 2].bitcast(BF16)
    qsum_bf = arE[0:NH, S + S // 2:2 * S].bitcast(BF16)
    out2_bf = arE[:, 0:2048].bitcast(BF16).rearrange("p (k t) -> p k t", k=KD)
    xq_q = arE[:, 2048:4096].bitcast(BF16).rearrange("p (k t) -> p k t", k=KD)
    o2sq8 = xq_q
    macc = mmx[:, 0:512]
    maccB = mmx[:, 512:1024]
    mincc = mmn[:, 0:512]
    minccB = mmn[:, 512:1024]
    halo = [arE[:, 4096 + i * 64:4096 + (i + 1) * 64]
            .rearrange("p (c w) -> p c w", c=KH) for i in range(2)]
    csb_sb = arE[:, 4224:4736]

    ROWS_t = sbuf("rows", [1, 10 * TG], F32)

    def row(i):
        return ROWS_t[0:1, i * TG:(i + 1) * TG]

    e16f = sbuf("e16f_s", [NH, KD * 128], F32)
    e16b = sbuf("e16b_s", [NH, KD * 128], BF16)
    eye = sbuf("eye_s", [128, 128], F32)
    eye_b = sbuf("eye_b", [128, 128], BF16)
    onesrow = sbuf("onesr", [1, 128], F32)
    onescol_f = sbuf("onescf", [128, 1], F32)
    onescol_b = sbuf("onescb", [128, 1], BF16)
    g_row = sbuf("g_rowS", [1, D], F32)
    lnb_row = sbuf("lnb_rowS", [1, D], F32)
    g128 = sbuf("g128S", [128, KD], F32)
    lnb128 = sbuf("lnb128S", [128, KD], F32)
    bo_sb = sbuf("bo_sb", [128, KD], F32)
    bbeta_sb = sbuf("bbeta_sb", [NH, 1], F32)
    convw_sb = sbuf("convw_s", [128, KH * 3], F32)
    convws = sbuf("convws", [128, KH * 3], F32)
    convb_sb = sbuf("convb_s", [128, KH], F32)
    wqs_f = sbuf("wqs_f", [128, KD * NH], F32)
    wqs_bf = sbuf("wqs_bf", [128, KD * NH], BF16)
    wvs_bf = sbuf("wvs_bf", [128, KD * NH], BF16)
    wbe_bf = sbuf("wbe_bf", [128, KD * NH], BF16)
    wn_ones = sbuf("wn_ones", [128, KD], BF16)
    wn_g = sbuf("wn_g", [128, KD], BF16)
    wn_g2 = sbuf("wn_g2", [128, KD], BF16)
    wn_gl = sbuf("wn_gl", [128, KD], BF16)
    tmp128 = sbuf("tmp128", [128, KD], F32)
    SCAL = sbuf("SCAL", [1, 16], F32)
    sw1col = sbuf("sw1col", [128, 2], F32)
    sw2col = sbuf("sw2col", [128, 2], F32)
    m1col = sbuf("m1col", [128, 1], F32)
    m2col = sbuf("m2col", [128, 1], F32)
    c128 = sbuf("c128", [128, 1], F32)
    cm128 = sbuf("cm128", [128, 1], F32)
    s127c = sbuf("s127c", [128, 1], F32)
    wrow = sbuf("wrow", [1, 260], F32)
    stt = sbuf("stt", [128, 32], F32)
    absred = sbuf("absred", [128, 17], F32)
    t0col = sbuf("t0col", [128, KD], BF16)
    t0u = sbuf("t0u", [128, KD], F32)
    t0d = sbuf("t0d", [128, KD], I32)

    PS = [ctx.enter_context(nc.psum_tensor(f"ps{i}", [128, TG], F32))
          for i in range(8)]

    bar = ctx.enter_context(nc.semaphore("bar"))
    ld = ctx.enter_context(nc.semaphore("ld"))
    st = ctx.enter_context(nc.semaphore("st"))
    stp = [ctx.enter_context(nc.semaphore("sta")),
           ctx.enter_context(nc.semaphore("stb"))]
    cnt = {"ld": 0, "st": 0, "sta": 0, "stb": 0}

    stages = []
    side = []
    cur = [stages]

    def to_side():
        cur[0] = side

    def to_main():
        cur[0] = stages

    def stage():
        cur[0].append({e: [] for e in ENGS})

    def Sy(fn):
        cur[0][-1]["sync"].append(fn)

    def Sv(fn):
        cur[0][-1]["vector"].append(fn)

    def Ss(fn):
        cur[0][-1]["scalar"].append(fn)

    def St(fn):
        cur[0][-1]["tensor"].append(fn)

    def Sg(fn):
        cur[0][-1]["gpsimd"].append(fn)

    def zip_side(anchor, limit):
        assert anchor + len(side) <= limit, (anchor, len(side), limit)
        for i, stg in enumerate(side):
            for e in ENGS:
                stages[anchor + i][e].extend(stg[e])
        side.clear()

    def splice_side(limit):
        stages[limit:limit] = list(side)
        side.clear()

    def dma_in(w, dst, src):
        cnt["ld"] += 16
        w.raw(lambda e: e.dma_start(out=dst, in_=src).then_inc(ld, 16))

    def dma_wait(w):
        w.wait(ld, cnt["ld"])

    def dma_out(w, dst, src):
        cnt["st"] += 16
        w.raw(lambda e: e.dma_start(out=dst, in_=src).then_inc(st, 16))

    def TT(w, out, a, b, op):
        w.do([out], [a, b], lambda e: e.tensor_tensor(out, a, b, op))

    def TS(w, out, a, s1, s2, op0, op1=OP.bypass):
        ins = [a] + [x for x in (s1, s2) if hasattr(x, "tensor")]
        w.do([out], ins, lambda e: e.tensor_scalar(out, a, s1, s2, op0, op1))

    def STT(w, out, in0, sc, in1, op0, op1):
        ins = [in0, in1] + ([sc] if hasattr(sc, "tensor") else [])
        w.do([out], ins,
             lambda e: e.scalar_tensor_tensor(out, in0, sc, in1, op0, op1))

    def CP(w, out, a):
        w.do([out], [a], lambda e: e.tensor_copy(out, a))

    def ACT(w, out, a, func, bias=0.0, scale=1.0):
        ins = [a] + [x for x in (bias, scale) if hasattr(x, "tensor")]
        w.do([out], ins,
             lambda e: e.activation(out, a, func, bias=bias, scale=scale))

    def RECIP(w, out, a):
        w.do([out], [a], lambda e: e.reciprocal(out, a))

    def MM(w, out, lhsT, rhs, start, stop):
        w.do([out], [lhsT, rhs],
             lambda e: e.matmul(out, lhsT, rhs, start=start, stop=stop),
             no_drain=not start)

    def TRN(w, out, a):
        w.do([out], [a, eye[:]], lambda e: e.transpose(out, a, eye[:]))

    def TRNB(w, out, a):
        w.do([out], [a, eye_b[:]], lambda e: e.transpose(out, a, eye_b[:]))

    def MEMS(w, out, v):
        w.do([out], [], lambda e: e.memset(out, v))

    def RMAX(w, out, a):
        w.do([out], [a], lambda e: e.reduce_max(out, a, axis=AX))

    def RMIN(w, out, a):
        w.do([out], [a], lambda e: e.tensor_reduce(out, a, AX, OP.min))

    def RSUM(w, out, a, absv=False):
        w.do([out], [a], lambda e: e.reduce_sum(
            out, a, axis=AX, apply_absolute_value=absv))

    xT_k = xT.rearrange("(k p) t -> k p t", p=128)
    w1T_k = w1T_d.rearrange("(k p) c -> k p c", p=128)
    w2T_k = w2T_d.rearrange("(k p) j -> k p j", p=128)
    w1s_k = w1t_dram.rearrange("(k p) c -> k p c", p=128)
    w2s_k = w2t_dram.rearrange("(k p) j -> k p j", p=128)
    outT_k = outT.rearrange("(k p) t -> k p t", p=128)
    attnw_k = {"wq": wq_d.rearrange("(k p) j -> k p j", p=128),
               "wk": wk_d.rearrange("(k p) j -> k p j", p=128),
               "wv": wv_d.rearrange("(k p) j -> k p j", p=128),
               "wo": wo_d.rearrange("(k p) j -> k p j", p=128)}
    wbeta_k = wbeta_d.rearrange("(k p) h -> k p h", p=128)

    stage()

    def ld_consts(w):
        for dst, src in [(e16f[:], e16f_d), (e16b[:], e16b_d), (eye[:], eye_d),
                         (onesrow[:], onesrow_d), (g_row[:], grow_d),
                         (lnb_row[:], lrow_d), (g128[:], g128_d),
                         (lnb128[:], lnb128_d), (bo_sb[:], bo_d),
                         (bbeta_sb[:], bbeta_d), (convw_sb[:], convw_d),
                         (convb_sb[:], convb_d)]:
            dma_in(w, dst, src)
        dma_wait(w)
    Sy(ld_consts)

    stage()

    def prep_consts(w):
        MEMS(w, stt[:], 0.0)
        MEMS(w, onescol_f[:], 1.0)
        MEMS(w, onescol_b[:], 1.0)
        MEMS(w, c128[:], 128.0)
        MEMS(w, cm128[:], -128.0)
        MEMS(w, s127c[:], 1.0 / 127.0)
        CP(w, eye_b[:], eye[:])
        MEMS(w, wn_ones[:], 1.0)
        CP(w, wn_g[:], g128[:])
        TT(w, stt[:, 0:KD], g128[:], g128[:], OP.mult)
        CP(w, wn_g2[:], stt[:, 0:KD])
        TT(w, stt[:, KD:2 * KD], g128[:], lnb128[:], OP.mult)
        CP(w, wn_gl[:], stt[:, KD:2 * KD])
        TT(w, stt[:, 2 * KD:3 * KD], lnb128[:], lnb128[:], OP.mult)
        MEMS(w, macc[:], 0.0)
        MEMS(w, mincc[:], 0.0)
    Sv(prep_consts)

    stage()

    def sconst_mm(w):
        srcs = [g128[:], stt[:, 0:KD], lnb128[:], stt[:, KD:2 * KD],
                stt[:, 2 * KD:3 * KD]]
        for ci, src in enumerate(srcs):
            for k in range(KD):
                MM(w, PS[0][0:1, ci:ci + 1], src[:, k:k + 1],
                   onescol_f[:, 0:1], k == 0, k == KD - 1)
    St(sconst_mm)
    stage()
    Ss(lambda w: ACT(w, SCAL[0:1, 0:5], PS[0][0:1, 0:5], AF.Copy))

    xsq = arC[:, TG:2 * TG]
    stage()
    Sy(lambda w: (dma_in(w, xst[0], xT_k[0]), dma_wait(w)))
    for k in range(KD):
        for tg in range(NTG):
            stage()
            if tg == 0 and k + 1 < KD:
                Sy(lambda w, k=k: (dma_in(w, xst[(k + 1) % 2], xT_k[k + 1]),
                                   dma_wait(w)))
            Ss(lambda w, k=k, tg=tg: ACT(
                w, xsq, xst[k % 2][:, tg * TG:(tg + 1) * TG], AF.Square))
            stage()

            def ln1mm(w, k=k, tg=tg):
                MM(w, PS[tg][0:1, 0:TG], onescol_f[:, 0:1],
                   xst[k % 2][:, tg * TG:(tg + 1) * TG], k == 0, k == KD - 1)
                MM(w, PS[4 + tg][0:1, 0:TG], onescol_f[:, 0:1], xsq,
                   k == 0, k == KD - 1)
            St(ln1mm)

    stage()

    def ln1_evac(w):
        for tg in range(NTG):
            ACT(w, row(tg), PS[tg][0:1, 0:TG], AF.Copy)
            ACT(w, row(4 + tg), PS[4 + tg][0:1, 0:TG], AF.Copy)
    Ss(ln1_evac)
    stage()

    def ln1_math1(w):
        for tg in range(NTG):
            TS(w, row(tg), row(tg), DINV, None, OP.mult)
            TS(w, row(4 + tg), row(4 + tg), DINV, None, OP.mult)
            TT(w, row(8), row(tg), row(tg), OP.mult)
            TT(w, row(4 + tg), row(4 + tg), row(8), OP.subtract)
            TS(w, row(4 + tg), row(4 + tg), EPS, None, OP.add)
            RECIP(w, row(4 + tg), row(4 + tg))
    Sv(ln1_math1)
    stage()
    Ss(lambda w: [ACT(w, row(4 + tg), row(4 + tg), AF.Sqrt)
                  for tg in range(NTG)])
    stage()

    def ln1_math2(w):
        for tg in range(NTG):
            TT(w, row(tg), row(tg), row(4 + tg), OP.mult)
            TS(w, row(tg), row(tg), -1.0, None, OP.mult)
    Sv(ln1_math2)

    stage()

    def ln1_bcast(w):
        for tg in range(NTG):
            MM(w, PS[tg][:, 0:TG], onesrow[0:1, :], row(4 + tg), True, True)
            MM(w, PS[4 + tg][:, 0:TG], onesrow[0:1, :], row(tg), True, True)
    St(ln1_bcast)
    stage()
    Sy(lambda w: (dma_in(w, xst[0], xT_k[0]), dma_wait(w)))

    def ln1_aff1(w, k):
        sl = (k % 2) * 4
        for tg in range(NTG):
            TT(w, t1buf[:, sl + tg, :], xst[k % 2][:, tg * TG:(tg + 1) * TG],
               PS[tg][:, 0:TG], OP.mult)

    def ln1_aff2(w, k):
        sl = (k % 2) * 4
        for tg in range(NTG):
            TT(w, xinq[:, k, tg * TG:(tg + 1) * TG], t1buf[:, sl + tg, :],
               PS[4 + tg][:, 0:TG], OP.add)

    for k in range(KD):
        stage()
        if k + 1 < KD:
            Sy(lambda w, k=k: (dma_in(w, xst[(k + 1) % 2], xT_k[k + 1]),
                               dma_wait(w)))
        Sv(lambda w, k=k: (ln1_aff1(w, k),
                           ln1_aff2(w, k - 1) if k > 0 else None))
    stage()
    Sv(lambda w: ln1_aff2(w, KD - 1))

    def wprep(name, per_chunk, sched=None, lag=None):
        srck = attnw_k[name] if name != "wbeta" else wbeta_k
        ncols = 1024 if name != "wbeta" else NH
        sched = sched if sched is not None else Sv
        stage()
        Sy(lambda w: (dma_in(w, xst[0][:, 0:ncols], srck[0]), dma_wait(w)))
        for k in range(KD):
            stage()
            if k + 1 < KD:
                Sy(lambda w, k=k: (dma_in(w, xst[(k + 1) % 2][:, 0:ncols],
                                          srck[k + 1]), dma_wait(w)))
            sched(lambda w, k=k: per_chunk(w, k, xst[k % 2]))
            if lag is not None and k > 0:
                Ss(lambda w, k=k: lag(w, k - 1))
        if lag is not None:
            stage()
            Ss(lambda w: lag(w, KD - 1))

    def wqs_chunk(w, k, src):
        RSUM(w, wqs_f[:, k * NH:(k + 1) * NH],
             src[:, 0:1024].rearrange("p (h j) -> p h j", j=HD))

    def wqs_lag(w, k):
        ACT(w, wqs_bf[:, k * NH:(k + 1) * NH],
            wqs_f[:, k * NH:(k + 1) * NH], AF.Copy)

    def wvs_chunk(w, k, src):
        RSUM(w, wqs_f[:, k * NH:(k + 1) * NH],
             src[:, 0:1024].rearrange("p (h j) -> p h j", j=HD))

    def wvs_lag(w, k):
        ACT(w, wvs_bf[:, k * NH:(k + 1) * NH],
            wqs_f[:, k * NH:(k + 1) * NH], AF.Copy)

    def wbe_chunk(w, k, src):
        CP(w, wbe_bf[:, k * NH:(k + 1) * NH], src[:, 0:NH])

    def wk_chunk(w, k, src):
        ACT(w, wk_bf[:, k, :], src[:, 0:1024], AF.Copy)

    wprep("wq", wqs_chunk, lag=wqs_lag)
    wprep("wv", wvs_chunk, lag=wvs_lag)
    wprep("wbeta", wbe_chunk)
    wprep("wk", wk_chunk, sched=Ss)

    for tg in range(NTG):
        stage()

        def thin_mm(w, tg=tg):
            for ci, lhs in enumerate([wqs_bf, wvs_bf, wbe_bf]):
                for k in range(KD):
                    MM(w, PS[ci][0:NH, 0:TG], lhs[:, k * NH:(k + 1) * NH],
                       xinq[:, k, tg * TG:(tg + 1) * TG], k == 0, k == KD - 1)
        St(thin_mm)
        stage()

        def thin_evac(w, tg=tg):
            sl = slice(tg * TG, (tg + 1) * TG)
            ACT(w, qsum_bf[:, sl], PS[0][0:NH, 0:TG], AF.Copy)
            ACT(w, vsum_bf[:, sl], PS[1][0:NH, 0:TG], AF.Copy)
            ACT(w, betaT[:, sl], PS[2][0:NH, 0:TG], AF.Sigmoid,
                bias=bbeta_sb[:, 0:1])
        Ss(thin_evac)

    for ct in range(KD):
        ef = e16f[:, ct * 128:(ct + 1) * 128]
        eb = e16b[:, ct * 128:(ct + 1) * 128]
        stage()
        St(lambda w, ef=ef: [MM(w, PS[tg][:, 0:TG], ef,
                                betaT[:, tg * TG:(tg + 1) * TG], True, True)
                             for tg in range(NTG)])
        stage()
        Ss(lambda w: [ACT(w, beta_b[:, tg * TG:(tg + 1) * TG],
                          PS[tg][:, 0:TG], AF.Copy) for tg in range(NTG)])
        stage()

        def scan_t2(w, eb=eb, ct=ct):
            for tg in range(NTG):
                MM(w, PS[tg][:, 0:TG], eb, vsum_bf[:, tg * TG:(tg + 1) * TG],
                   True, True)
            for tg in range(NTG):
                for k in range(KD):
                    MM(w, PS[4 + tg][:, 0:TG],
                       wk_bf[:, k, ct * 128:(ct + 1) * 128],
                       xinq[:, k, tg * TG:(tg + 1) * TG], k == 0, k == KD - 1)
        St(scan_t2)
        stage()
        Ss(lambda w: [ACT(w, vsqs_b[:, tg * TG:(tg + 1) * TG],
                          PS[tg][:, 0:TG], AF.Copy) for tg in range(NTG)])
        stage()
        Sv(lambda w: [TT(w, u_sc[:, tg * TG:(tg + 1) * TG], PS[4 + tg][:, 0:TG],
                         vsqs_b[:, tg * TG:(tg + 1) * TG], OP.mult)
                      for tg in range(NTG)])
        stage()
        Sv(lambda w: w.do([g_sc], [beta_b, u_sc],
                          lambda e: e.tensor_tensor_scan(
                              g_sc, beta_b, u_sc, 0.0, OP.mult, OP.add)))
        St(lambda w, eb=eb: [MM(w, PS[tg][:, 0:TG], eb,
                                qsum_bf[:, tg * TG:(tg + 1) * TG], True, True)
                             for tg in range(NTG)])
        stage()
        Sv(lambda w, ct=ct: [TT(w, oTv[:, ct, tg * TG:(tg + 1) * TG],
                                g_sc[:, tg * TG:(tg + 1) * TG],
                                PS[tg][:, 0:TG], OP.mult)
                             for tg in range(NTG)])

    if debug:
        stage()

        def dbg1(w):
            for c in range(KD):
                dma_out(w, dbg["dbg_oT"].rearrange("(k p) t -> k p t", p=128)[c],
                        oTv[:, c, :])
                dma_out(w, dbg["dbg_xinq"].rearrange(
                    "(k p) t -> k p t", p=128)[c], xinq[:, c, :])
            w.wait(st, cnt["st"])
        Sy(dbg1)

    stage()
    Sy(lambda w: (dma_in(w, xst[0][:, 0:1024], attnw_k["wo"][0]), dma_wait(w)))
    for k in range(KD):
        stage()
        if k + 1 < KD:
            Sy(lambda w, k=k: (dma_in(w, xst[(k + 1) % 2][:, 0:1024],
                                      attnw_k["wo"][k + 1]), dma_wait(w)))
        Ss(lambda w, k=k: ACT(w, wo_bf[:, k, :], xst[k % 2][:, 0:1024],
                              AF.Copy))

    for tg in range(NTG):
        stage()

        def wo_mm(w, tg=tg):
            for jt in range(KD):
                for c in range(KD):
                    MM(w, PS[jt][:, 0:TG], wo_bf[:, c, jt * 128:(jt + 1) * 128],
                       oTv[:, c, tg * TG:(tg + 1) * TG], c == 0, c == KD - 1)
        St(wo_mm)
        stage()
        Sv(lambda w, tg=tg: [STT(w, Yv[:, jt, tg * TG:(tg + 1) * TG],
                                 PS[jt][:, 0:TG], bo_sb[:, jt:jt + 1],
                                 xinq[:, jt, tg * TG:(tg + 1) * TG],
                                 OP.add, OP.add) for jt in range(KD)])

    if debug:
        stage()

        def dbg2(w):
            for k in range(KD):
                dma_out(w, dbg["dbg_y"].rearrange("(k p) t -> k p t", p=128)[k],
                        Yv[:, k, :])
            w.wait(st, cnt["st"])
        Sy(dbg2)

    lnab_mark = len(stages)
    for tg in range(NTG):
        stage()
        Ss(lambda w, tg=tg: [ACT(w, ysq[:, k, :],
                                 Yv[:, k, tg * TG:(tg + 1) * TG], AF.Square)
                             for k in range(KD)])
        stage()

        def lnab_mm(w, tg=tg):
            cfg = [(wn_ones, "y"), (wn_g, "y"), (wn_g2, "y"), (wn_gl, "y"),
                   (wn_ones, "q"), (wn_g2, "q")]
            for ci, (lhs, rv) in enumerate(cfg):
                for k in range(KD):
                    rhs = (Yv[:, k, tg * TG:(tg + 1) * TG] if rv == "y"
                           else ysq[:, k, :])
                    MM(w, PS[ci][0:1, 0:TG], lhs[:, k:k + 1], rhs,
                       k == 0, k == KD - 1)
        St(lnab_mm)
        stage()
        Ss(lambda w: [ACT(w, row(i), PS[i][0:1, 0:TG], AF.Copy)
                      for i in range(6)])
        stage()

        def lnab_m1(w):
            TS(w, row(0), row(0), DINV, None, OP.mult)
            TS(w, row(4), row(4), DINV, None, OP.mult)
            TT(w, row(6), row(0), row(0), OP.mult)
            TT(w, row(4), row(4), row(6), OP.subtract)
            TS(w, row(4), row(4), EPS, None, OP.add)
            RECIP(w, row(4), row(4))
        Sv(lnab_m1)
        stage()
        Ss(lambda w: ACT(w, row(4), row(4), AF.Sqrt))
        stage()

        def lnab_m2(w):
            TS(w, row(6), row(0), SCAL[0:1, 0:1], None, OP.mult)
            TT(w, row(1), row(1), row(6), OP.subtract)
            TT(w, row(1), row(1), row(4), OP.mult)
            TS(w, row(1), row(1), SCAL[0:1, 2:3], None, OP.add)
            TS(w, row(1), row(1), DINV, None, OP.mult)
            TS(w, row(6), row(0), SCAL[0:1, 1:2], None, OP.mult)
            TT(w, row(6), row(6), row(0), OP.mult)
            TS(w, row(8), row(2), -2.0, None, OP.mult)
            TT(w, row(8), row(8), row(0), OP.mult)
            TT(w, row(5), row(5), row(6), OP.add)
            TT(w, row(5), row(5), row(8), OP.add)
            TT(w, row(8), row(4), row(4), OP.mult)
            TT(w, row(5), row(5), row(8), OP.mult)
            TS(w, row(8), row(0), SCAL[0:1, 3:4], None, OP.mult)
            TT(w, row(8), row(3), row(8), OP.subtract)
            TT(w, row(8), row(8), row(4), OP.mult)
            TS(w, row(8), row(8), 2.0, None, OP.mult)
            TT(w, row(5), row(5), row(8), OP.add)
            TS(w, row(5), row(5), SCAL[0:1, 4:5], None, OP.add)
            TS(w, row(5), row(5), DINV, None, OP.mult)
            TT(w, row(8), row(1), row(1), OP.mult)
            TT(w, row(5), row(5), row(8), OP.subtract)
            TS(w, row(5), row(5), EPS, None, OP.add)
            RECIP(w, row(5), row(5))
        Sv(lnab_m2)
        stage()
        Ss(lambda w: ACT(w, row(7), row(5), AF.Sqrt))
        stage()

        def lnab_m3(w):
            TT(w, row(6), row(4), row(7), OP.mult)
            TT(w, row(8), row(0), row(6), OP.mult)
            TS(w, row(8), row(8), -1.0, None, OP.mult)
            TT(w, row(9), row(1), row(7), OP.mult)
            TS(w, row(9), row(9), -1.0, None, OP.mult)
        Sv(lnab_m3)
        for half in range(2):
            ks = list(range(half * 4, half * 4 + 4))
            stage()

            def x_half_load(w, ks=ks, tg=tg):
                for i, k in enumerate(ks):
                    dma_in(w, xst[i % 2][:, (i // 2) * TG:(i // 2 + 1) * TG],
                           xT_k[k][:, tg * TG:(tg + 1) * TG])
                dma_wait(w)
            Sy(x_half_load)

            def lnab_bc(w, ks=ks):
                for i, k in enumerate(ks):
                    MM(w, PS[2 * i][:, 0:TG],
                       g_row[0:1, k * 128:(k + 1) * 128], row(6), True, True)
                    MM(w, PS[2 * i + 1][:, 0:TG],
                       lnb_row[0:1, k * 128:(k + 1) * 128], row(7), True, False)
                    MM(w, PS[2 * i + 1][:, 0:TG],
                       g_row[0:1, k * 128:(k + 1) * 128], row(8), False, False)
                    MM(w, PS[2 * i + 1][:, 0:TG], onesrow[0:1, :], row(9),
                       False, True)
            St(lnab_bc)
            stage()

            def lnab_x2a(w, ks=ks, tg=tg):
                for i, k in enumerate(ks):
                    TT(w, pbuf[i], Yv[:, k, tg * TG:(tg + 1) * TG],
                       PS[2 * i][:, 0:TG], OP.mult)
            Sv(lnab_x2a)
            stage()
            Sv(lambda w, ks=ks: [TT(w, pbuf[i], pbuf[i], PS[2 * i + 1][:, 0:TG],
                                    OP.add) for i in range(4)])
            stage()

            def lnab_x2c(w, ks=ks, tg=tg):
                for i, k in enumerate(ks):
                    xc = xst[i % 2][:, (i // 2) * TG:(i // 2 + 1) * TG]
                    TT(w, x2bf[:, k, tg * TG:(tg + 1) * TG], pbuf[i], xc,
                       OP.add)
            Sv(lnab_x2c)

    if debug:
        stage()

        def dbg3(w):
            for k in range(KD):
                dma_out(w, dbg["dbg_x2"].rearrange("(k p) t -> k p t", p=128)[k],
                        x2bf[:, k, :])
            w.wait(st, cnt["st"])
        Sy(dbg3)

    PKEY = ("sta", "stb")
    bld = {"sta": 0, "stb": 0}

    def dma_out_p(w, dst, src, p):
        cnt[PKEY[p]] += 16
        w.raw(lambda e: e.dma_start(out=dst, in_=src).then_inc(stp[p], 16))

    x2f = x2bf_t[:].bitcast(F32)
    x2braw = x2bf_t[:]
    wst = [x2f[:, 0:2048], x2f[:, 2048:4096]]
    wpA2 = x2f[:, 4096:5120]
    wpB2 = x2f[:, 5120:6144]
    wpT2 = [x2braw[:, 12288:13312], x2braw[:, 13312:14336]]

    to_side()

    def wsum2(chunks, mcol, swcol, dstview=None):
        dv = dstview if dstview is not None else (lambda buf: buf)
        nch = len(chunks)
        stage()
        Sy(lambda w: (dma_in(w, dv(wst[0]), chunks[0]), dma_wait(w)))
        for ch in range(nch):
            stage()
            if ch + 1 < nch:
                Sy(lambda w, ch=ch: (dma_in(w, dv(wst[(ch + 1) % 2]),
                                            chunks[ch + 1]), dma_wait(w)))
            Sv(lambda w, ch=ch: RSUM(w, absred[:, ch:ch + 1], wst[ch % 2],
                                     absv=True))
        stage()
        Sv(lambda w, nch=nch: RSUM(w, absred[:, 16:17], absred[:, 0:nch]))
        stage()
        Sy(lambda w: (dma_in(w, wrow[0:1, 0:128], absred[:, 16:17]),
                      dma_wait(w)))
        stage()

        def wmean_v(w):
            RSUM(w, wrow[0:1, 128:129], wrow[0:1, 0:128])
            TS(w, wrow[0:1, 129:130], wrow[0:1, 128:129], 1.0 / (D * DH),
               None, OP.mult)
            TS(w, wrow[0:1, 129:130], wrow[0:1, 129:130], 1e-5, None, OP.max)
            RECIP(w, wrow[0:1, 130:131], wrow[0:1, 129:130])
            TS(w, wrow[0:1, 132:260], onesrow[0:1, :], wrow[0:1, 129:130],
               None, OP.mult)
        Sv(wmean_v)
        stage()
        Sy(lambda w: (dma_in(w, mcol[:, 0:1], wrow[0:1, 132:260]),
                      dma_wait(w)))
        stage()
        Sv(lambda w: TS(w, wrow[0:1, 132:260], onesrow[0:1, :],
                        wrow[0:1, 130:131], None, OP.mult))
        stage()
        Sy(lambda w: (dma_in(w, swcol[:, 0:1], wrow[0:1, 132:260]),
                      dma_wait(w)))
        stage()
        Sv(lambda w: TS(w, swcol[:, 1:2], swcol[:, 0:1], -1.0, None, OP.mult))

    def tern2(chunks, dsts, swcol):
        nch = len(chunks)
        stage()
        Sy(lambda w: (dma_in(w, wst[0][:, 0:1024], chunks[0]), dma_wait(w)))
        for ch in range(nch + 1):
            stage()
            if ch < nch:
                if ch + 1 < nch:
                    Sy(lambda w, ch=ch: (
                        dma_in(w, wst[(ch + 1) % 2][:, 0:1024],
                               chunks[ch + 1]), dma_wait(w)))
                tgt = bld[PKEY[ch % 2]]

                def tern_g(w, ch=ch, tgt=tgt):
                    if ch >= 2:
                        w.wait(stp[ch % 2], tgt)
                    TS(w, wpA2, wst[ch % 2][:, 0:1024], swcol[:, 0:1], 0.5,
                       OP.mult, OP.is_gt)
                    TS(w, wpB2, wst[ch % 2][:, 0:1024], swcol[:, 1:2], 0.5,
                       OP.mult, OP.is_gt)
                    TT(w, wpT2[ch % 2], wpA2, wpB2, OP.subtract)
                Sv(tern_g)
            if ch > 0:
                bld[PKEY[(ch - 1) % 2]] += 16
                Sy(lambda w, ch=ch: dma_out_p(
                    w, dsts[ch - 1], wpT2[(ch - 1) % 2], (ch - 1) % 2))
        stage()
        tgt_a, tgt_b = bld["sta"], bld["stb"]
        Sy(lambda w: (w.wait(stp[0], tgt_a), w.wait(stp[1], tgt_b)))

    w1srcs = [w1T_k[kk][:, hh * 2048:(hh + 1) * 2048]
              for kk in range(KD) for hh in range(2)]
    w2srcs = [w2T_d.rearrange("(a b p) j -> a p b j", b=2, p=128)[i]
              for i in range(16)]
    wsum2(w1srcs, m1col, sw1col)
    wsum2([s for s in w2srcs], m2col, sw2col,
          dstview=lambda buf: buf.rearrange("p (b j) -> p b j", b=2))
    stage()
    Sv(lambda w: TS(w, convws[:], convw_sb[:], m1col[:, 0:1], None, OP.mult))
    to_main()
    zip_side(1, lnab_mark)
    to_side()
    w1tsrc = [w1T_k[kk][:, hh * 1024:(hh + 1) * 1024]
              for kk in range(KD) for hh in range(4)]
    w1tdst = [w1s_k[kk][:, hh * 1024:(hh + 1) * 1024]
              for kk in range(KD) for hh in range(4)]
    tern2(w1tsrc, w1tdst, sw1col)
    tern2([w2T_k[kk] for kk in range(KH)],
          [w2s_k[kk] for kk in range(KH)], sw2col)
    to_main()
    splice_side(lnab_mark)

    tok0 = x2bf[:, :, 0:1]
    stage()
    Ss(lambda w: [ACT(w, ysq[:, k, 0:1], x2bf[:, k, 0:1], AF.Square)
                  for k in range(KD)])
    stage()

    def t0_mm(w):
        for k in range(KD):
            MM(w, PS[0][0:1, 0:1], wn_ones[:, k:k + 1], x2bf[:, k, 0:1],
               k == 0, k == KD - 1)
        for k in range(KD):
            MM(w, PS[1][0:1, 0:1], wn_ones[:, k:k + 1], ysq[:, k, 0:1],
               k == 0, k == KD - 1)
    St(t0_mm)
    stage()

    def t0_maxmin(w):
        CP(w, stt[:, 8:9], x2bf[:, 0, 0:1])
        CP(w, stt[:, 9:10], x2bf[:, 0, 0:1])
        for k in range(1, KD):
            TT(w, stt[:, 8:9], stt[:, 8:9], x2bf[:, k, 0:1], OP.max)
            TT(w, stt[:, 9:10], stt[:, 9:10], x2bf[:, k, 0:1], OP.min)
    Sv(t0_maxmin)
    stage()

    def t0_trn(w):
        TRN(w, PS[2][0:1, 0:128], stt[:, 8:9])
        TRN(w, PS[3][0:1, 0:128], stt[:, 9:10])
    St(t0_trn)
    stage()

    def t0_red(w):
        RMAX(w, row(6)[0:1, 0:1], PS[2][0:1, 0:128])
        RMIN(w, row(6)[0:1, 1:2], PS[3][0:1, 0:128])
    Sv(t0_red)
    stage()
    Ss(lambda w: ACT(w, row(6)[0:1, 2:4], PS[0][0:1, 0:1], AF.Copy)
       if False else [ACT(w, row(6)[0:1, 2:3], PS[0][0:1, 0:1], AF.Copy),
                      ACT(w, row(6)[0:1, 3:4], PS[1][0:1, 0:1], AF.Copy)])
    stage()

    def t0_math1(w):
        r = row(6)
        TS(w, r[0:1, 2:3], r[0:1, 2:3], DINV, None, OP.mult)
        TS(w, r[0:1, 3:4], r[0:1, 3:4], DINV, None, OP.mult)
        TT(w, r[0:1, 4:5], r[0:1, 2:3], r[0:1, 2:3], OP.mult)
        TT(w, r[0:1, 3:4], r[0:1, 3:4], r[0:1, 4:5], OP.subtract)
        TS(w, r[0:1, 3:4], r[0:1, 3:4], EPS, None, OP.add)
        RECIP(w, r[0:1, 3:4], r[0:1, 3:4])
    Sv(t0_math1)
    stage()
    Ss(lambda w: ACT(w, row(6)[0:1, 3:4], row(6)[0:1, 3:4], AF.Sqrt))
    stage()

    def t0_math2(w):
        r = row(6)
        TT(w, r[0:1, 0:1], r[0:1, 0:1], r[0:1, 2:3], OP.subtract)
    Sv(t0_math2)
    stage()

    def t0_math3(w):
        r = row(6)
        TT(w, r[0:1, 1:2], r[0:1, 2:3], r[0:1, 1:2], OP.subtract)
        TT(w, r[0:1, 0:1], r[0:1, 0:1], r[0:1, 1:2], OP.max)
        TT(w, r[0:1, 0:1], r[0:1, 0:1], r[0:1, 3:4], OP.mult)
        TS(w, r[0:1, 0:1], r[0:1, 0:1], 1e-5, None, OP.max)
        TS(w, r[0:1, 4:5], r[0:1, 0:1], 1.0 / 127.0, None, OP.mult)
        RECIP(w, r[0:1, 1:2], r[0:1, 0:1])
        TS(w, r[0:1, 1:2], r[0:1, 1:2], 127.0, None, OP.mult)
        TT(w, r[0:1, 0:1], r[0:1, 3:4], r[0:1, 1:2], OP.mult)
        TT(w, r[0:1, 1:2], r[0:1, 2:3], r[0:1, 0:1], OP.mult)
        TS(w, r[0:1, 1:2], r[0:1, 1:2], -1.0, None, OP.mult)
        TS(w, r[0:1, 1:2], r[0:1, 1:2], 128.0, None, OP.add)
    Sv(t0_math3)
    stage()
    St(lambda w: [MM(w, PS[4][:, i:i + 1], onesrow[0:1, :],
                     row(6)[0:1, i:i + 1], True, True) for i in range(3)])
    stage()
    Ss(lambda w: ACT(w, stt[:, 12:15], PS[4][:, 0:3], AF.Copy))
    stage()

    def t0_quant(w):
        TS(w, t0u[:], tok0.rearrange("p k x -> p (k x)"), stt[:, 12:13],
           stt[:, 13:14], OP.mult, OP.add)
        CP(w, t0d[:], t0u[:])
    Sv(t0_quant)
    stage()
    Sv(lambda w: (TT(w, t0u[:], t0u[:], t0d[:], OP.subtract),
                  STT(w, t0u[:], t0u[:], 0.5, t0d[:], OP.is_gt, OP.add)))
    stage()
    St(lambda w: MM(w, PS[4][:, 4:5], onesrow[0:1, :], row(6)[0:1, 4:5],
                    True, True))
    stage()
    Ss(lambda w: ACT(w, stt[:, 14:15], PS[4][:, 4:5], AF.Copy))
    stage()
    Sv(lambda w: TS(w, t0col[:], t0u[:], 128.0, stt[:, 14:15],
                    OP.subtract, OP.mult))

    for q in range(NTG):
        rs_ = q * TG + 1
        wr = TG if q < 3 else TG - 1
        cs_ = q * TG
        rsl = slice(rs_, rs_ + wr)
        csl = slice(cs_, cs_ + TG)

        stage()
        Sy(lambda w: w.wait(st, cnt["st"]))

        def mm3v(w, ka, kb, rsl=rsl, wr=wr):
            for acc, k in ((macc, ka), (maccB, kb)):
                if k < 2:
                    CP(w, acc[0:128, 0:wr], x2bf[:, k, rsl])
                else:
                    TT(w, acc[0:128, 0:wr], acc[0:128, 0:wr],
                       x2bf[:, k, rsl], OP.max)

        def mm3g(w, ka, kb, rsl=rsl, wr=wr):
            for acc, k in ((mincc, ka), (minccB, kb)):
                TT(w, acc[0:128, 0:wr], acc[0:128, 0:wr],
                   x2bf[:, k, rsl], OP.min)

        stage()
        Ss(lambda w, rsl=rsl, wr=wr: [ACT(w, x2sq[:, k, 0:wr],
                                          x2bf[:, k, rsl], AF.Square)
                                      for k in range(KD)])
        Sv(lambda w, rsl=rsl, wr=wr: (mm3v(w, 0, 1),
                                      CP(w, mincc[0:128, 0:wr],
                                         x2bf[:, 0, rsl]),
                                      CP(w, minccB[0:128, 0:wr],
                                         x2bf[:, 1, rsl])))
        stage()

        def ln3_mm(w, rsl=rsl, wr=wr):
            for k in range(KD):
                MM(w, PS[0][0:1, 0:wr], wn_ones[:, k:k + 1], x2bf[:, k, rsl],
                   k == 0, k == KD - 1)
            for k in range(KD):
                MM(w, PS[1][0:1, 0:wr], wn_ones[:, k:k + 1], x2sq[:, k, 0:wr],
                   k == 0, k == KD - 1)
        St(ln3_mm)
        Sv(lambda w: (mm3v(w, 2, 3), mm3g(w, 2, 3)))
        stage()
        Sv(lambda w: (mm3v(w, 4, 5), mm3g(w, 4, 5)))
        stage()
        Ss(lambda w, wr=wr: [ACT(w, row(0)[0:1, 0:wr], PS[0][0:1, 0:wr],
                                 AF.Copy),
                             ACT(w, row(1)[0:1, 0:wr], PS[1][0:1, 0:wr],
                                 AF.Copy)])
        Sv(lambda w: (mm3v(w, 6, 7), mm3g(w, 6, 7)))
        stage()
        Sv(lambda w, wr=wr: (TT(w, macc[0:128, 0:wr], macc[0:128, 0:wr],
                                maccB[0:128, 0:wr], OP.max),
                             TT(w, mincc[0:128, 0:wr], mincc[0:128, 0:wr],
                                minccB[0:128, 0:wr], OP.min)))
        stage()

        def ln3_m1(w, wr=wr):
            TS(w, row(0)[0:1, 0:wr], row(0)[0:1, 0:wr], DINV, None, OP.mult)
            TS(w, row(1)[0:1, 0:wr], row(1)[0:1, 0:wr], DINV, None, OP.mult)
            TT(w, row(2)[0:1, 0:wr], row(0)[0:1, 0:wr], row(0)[0:1, 0:wr],
               OP.mult)
            TT(w, row(1)[0:1, 0:wr], row(1)[0:1, 0:wr], row(2)[0:1, 0:wr],
               OP.subtract)
            TS(w, row(1)[0:1, 0:wr], row(1)[0:1, 0:wr], EPS, None, OP.add)
            RECIP(w, row(1)[0:1, 0:wr], row(1)[0:1, 0:wr])
        Sv(ln3_m1)
        stage()
        Ss(lambda w, wr=wr: [ACT(w, row(1)[0:1, 0:wr], row(1)[0:1, 0:wr],
                                 AF.Sqrt),
                             ACT(w, xst[0][:, 0:wr], macc[0:128, 0:wr],
                                 AF.Copy),
                             ACT(w, xst[0][:, 512:512 + wr],
                                 mincc[0:128, 0:wr], AF.Copy)])
        stage()

        def ln3_trn(w, wr=wr):
            nb = (wr + 127) // 128
            for b in range(nb):
                bw = min(128, wr - b * 128)
                TRN(w, PS[2][0:bw, b * 128:b * 128 + 128],
                    xst[0][:, b * 128:b * 128 + bw])
                TRN(w, PS[3][0:bw, b * 128:b * 128 + 128],
                    xst[0][:, 512 + b * 128:512 + b * 128 + bw])
                MM(w, PS[4][0:bw, b:b + 1],
                   row(0)[0:1, b * 128:b * 128 + bw], onesrow[0:1, 0:1],
                   True, True)
                MM(w, PS[4][0:bw, 4 + b:5 + b],
                   row(1)[0:1, b * 128:b * 128 + bw], onesrow[0:1, 0:1],
                   True, True)
        St(ln3_trn)
        stage()

        def ln3_tm(w, wr=wr):
            nb = (wr + 127) // 128
            for b in range(nb):
                bw = min(128, wr - b * 128)
                RMAX(w, stt[0:bw, b:b + 1],
                     PS[2][0:bw, b * 128:b * 128 + 128])
                RMIN(w, stt[0:bw, 4 + b:5 + b],
                     PS[3][0:bw, b * 128:b * 128 + 128])
                CP(w, stt[0:bw, 8 + b:9 + b], PS[4][0:bw, b:b + 1])
                CP(w, stt[0:bw, 12 + b:13 + b], PS[4][0:bw, 4 + b:5 + b])
        Sv(ln3_tm)
        stage()

        def ln3_tmmath(w, wr=wr):
            nb = (wr + 127) // 128
            c = stt
            TT(w, c[:, 0:nb], c[:, 0:nb], c[:, 8:8 + nb], OP.subtract)
            TT(w, c[:, 4:4 + nb], c[:, 8:8 + nb], c[:, 4:4 + nb], OP.subtract)
            TT(w, c[:, 0:nb], c[:, 0:nb], c[:, 4:4 + nb], OP.max)
            TT(w, c[:, 0:nb], c[:, 0:nb], c[:, 12:12 + nb], OP.mult)
            TS(w, c[:, 0:nb], c[:, 0:nb], 1e-5, None, OP.max)
            TS(w, c[:, 16:16 + nb], c[:, 0:nb], 1.0 / 127.0, None, OP.mult)
            RECIP(w, c[:, 4:4 + nb], c[:, 0:nb])
            TS(w, c[:, 4:4 + nb], c[:, 4:4 + nb], 127.0, None, OP.mult)
            TT(w, c[:, 20:20 + nb], c[:, 12:12 + nb], c[:, 4:4 + nb],
               OP.mult)
            TT(w, c[:, 24:24 + nb], c[:, 8:8 + nb], c[:, 20:20 + nb], OP.mult)
            TS(w, c[:, 24:24 + nb], c[:, 24:24 + nb], -1.0, None, OP.mult)
            TS(w, c[:, 24:24 + nb], c[:, 24:24 + nb], 128.0, None, OP.add)
        Sv(ln3_tmmath)
        stage()

        def ln3_rowify(w, wr=wr):
            nb = (wr + 127) // 128
            for pi, cbase in [(5, 20), (6, 24), (7, 16)]:
                for b in range(nb):
                    TRN(w, PS[pi][0:1, b * 128:b * 128 + 128],
                        stt[0:128, cbase + b:cbase + b + 1])
        St(ln3_rowify)
        stage()
        Ss(lambda w, wr=wr: [ACT(w, row(3)[0:1, 0:wr], PS[5][0:1, 0:wr],
                                 AF.Copy),
                             ACT(w, row(4)[0:1, 0:wr], PS[6][0:1, 0:wr],
                                 AF.Copy),
                             ACT(w, row(5)[0:1, 0:wr], PS[7][0:1, 0:wr],
                                 AF.Copy)])
        stage()

        def ln3_bcast(w, wr=wr):
            MM(w, PS[0][:, 0:wr], onesrow[0:1, :], row(3)[0:1, 0:wr],
               True, True)
            MM(w, PS[1][:, 0:wr], onesrow[0:1, :], row(4)[0:1, 0:wr],
               True, True)
            MM(w, PS[2][:, 0:wr], onesrow[0:1, :], row(5)[0:1, 0:wr],
               True, True)
        St(ln3_bcast)
        if debug and q == 0:
            stage()

            def dbgps_fn(w):
                ACT(w, csb_sb[:, 0:8], PS[0][:, 504:512], AF.Copy)
                ACT(w, csb_sb[:, 8:16], PS[1][:, 504:512], AF.Copy)
                ACT(w, csb_sb[:, 16:24], PS[2][:, 504:512], AF.Copy)
            Ss(dbgps_fn)
            stage()
            Sy(lambda w: (dma_out(w, dbg["dbg_ps"], csb_sb[:, 0:24]),
                          w.wait(st, cnt["st"])))
        stage()

        def qx_u(w, rsl=rsl, wr=wr):
            for k in range(KD):
                TT(w, t1buf[:, k, 0:wr], x2bf[:, k, rsl], PS[0][:, 0:wr],
                   OP.mult)
        Sv(qx_u)
        stage()
        Sv(lambda w, wr=wr: [TT(w, u0buf[:, k, 0:wr], t1buf[:, k, 0:wr],
                                PS[1][:, 0:wr], OP.add) for k in range(KD)])
        if debug and q == 0:
            Sv(lambda w: CP(w, arB_t[:][:, 8192:8704], u0buf[:, 0, :]))
        stage()
        Ss(lambda w, wr=wr: [ACT(w, dbuf[:, k, 0:wr], u0buf[:, k, 0:wr],
                                 AF.Identity) for k in range(KD)])
        stage()
        if debug and q == 0:
            Sv(lambda w: CP(w, arB_t[:][:, 8704:9216], dbuf[:, 0, :]))
            stage()

        def qx_r(w, wr=wr):
            for k in range(KD):
                TT(w, u0buf[:, k, 0:wr], u0buf[:, k, 0:wr], dbuf[:, k, 0:wr],
                   OP.subtract)
        Sv(qx_r)
        stage()
        Sv(lambda w, wr=wr: [STT(w, u0buf[:, k, 0:wr], u0buf[:, k, 0:wr], 0.5,
                                 dbuf[:, k, 0:wr], OP.is_gt, OP.add)
                             for k in range(KD)])
        stage()
        Sv(lambda w, wr=wr: [STT(w, xq_q[:, k, 0:wr], u0buf[:, k, 0:wr],
                                 -128.0, PS[2][:, 0:wr], OP.add, OP.mult)
                             for k in range(KD)])
        if debug and q == 0:
            Sv(lambda w: (CP(w, arB_t[:][:, 9216:9728], u0buf[:, 0, :]),
                          CP(w, arB_t[:][:, 9728:10240], xq_q[:, 0, :])))

        if debug and q == 0:
            stage()

            def dbg4(w):
                for k in range(KD):
                    dma_out(w, dbg["dbg_xqv"].rearrange(
                        "(k p) t -> k p t", p=128)[k][:, 0:wr],
                        xq_q[:, k, 0:wr])
                dma_out(w, dbg["dbg_stt"], stt[:])
                dma_out(w, dbg["dbg_q"], arB_t[:][:, 8192:10240])
                dma_out(w, dbg["dbg_rows"], ROWS_t[0:1, :])
                w.wait(st, cnt["st"])
            Sy(dbg4)

        def ldw1g(w, bi, cg):
            dst = arB_bf[:, KD * S + bi * 8192:KD * S + (bi + 1) * 8192] \
                .rearrange("p (k c) -> p k c", k=KD)
            src = w1t_dram.rearrange("(k p) c -> p k c", p=128) \
                [:, :, cg * 1024:(cg + 1) * 1024]
            dma_in(w, dst, src)
            dma_wait(w)
        stage()
        Sy(lambda w: ldw1g(w, 0, 0))
        if q == 0:
            stage()
            Sv(lambda w: MEMS(w, arE[:, 4096:4160], 0.0))

        hq_v, hq_g = [], []

        def mk_hmm(ct, acc, op):
            first = ct < 2

            def f(w, ct=ct, acc=acc, op=op, first=first):
                if first:
                    CP(w, acc, hs_q[:, ct, :])
                else:
                    TT(w, acc, acc, hs_q[:, ct, :], op)
            return f

        def drip():
            for _ in range(4):
                if hq_v:
                    Sv(hq_v.pop(0))

        for cg in range(4):
            cts = list(range(cg * 8, cg * 8 + 8))
            if q == 0:
                stage()

                def strip_mm(w, cts=cts, cg=cg):
                    for i, ct in enumerate(cts):
                        for k in range(KD):
                            MM(w, PS[7][:, i:i + 1],
                               w1g[cg % 2][:, k, (ct % 8) * 128:
                                           (ct % 8) * 128 + 128],
                               t0col[:, k:k + 1], k == 0, k == KD - 1)
                St(strip_mm)
                drip()
                stage()
                Ss(lambda w, cts=cts: [ACT(w, halo[0][:, ct, 1:2],
                                           PS[7][:, i:i + 1], AF.Copy)
                                       for i, ct in enumerate(cts)])
                drip()
            stage()

            def conv_mm(w, cts=cts, cg=cg, rsl=rsl, wr=wr):
                for i, ct in enumerate(cts):
                    for k in range(KD):
                        MM(w, PS[i][:, 0:wr],
                           w1g[cg % 2][:, k, (ct % 8) * 128:(ct % 8) * 128 + 128],
                           xq_q[:, k, 0:wr], k == 0, k == KD - 1)
            St(conv_mm)
            if cg + 1 < 4:
                Sy(lambda w, cg=cg: ldw1g(w, (cg + 1) % 2, cg + 1))
            drip()
            stage()

            def conv_center(w, cts=cts, wr=wr):
                for i, ct in enumerate(cts):
                    ACT(w, halo[1][:, ct, 0:2], PS[i][:, wr - 2:wr], AF.Copy)
                    ACT(w, t1buf[:, i, 1:TG], PS[i][:, 0:TG - 1], AF.Identity,
                        bias=convb_sb[:, ct:ct + 1],
                        scale=convws[:, ct * 3 + 1:ct * 3 + 2])
                    ACT(w, t1buf[:, i, 0:1], halo[0][:, ct, 1:2], AF.Identity,
                        bias=convb_sb[:, ct:ct + 1],
                        scale=convws[:, ct * 3 + 1:ct * 3 + 2])
            Ss(conv_center)
            drip()
            stage()

            def conv_lrA(w, cts=cts, wr=wr):
                for i, ct in enumerate(cts):
                    STT(w, t1buf[:, i, 2:TG], PS[i][:, 0:TG - 2],
                        convws[:, ct * 3:ct * 3 + 1], t1buf[:, i, 2:TG],
                        OP.mult, OP.add)
                    STT(w, t1buf[:, i, 0:2], halo[0][:, ct, 0:2],
                        convws[:, ct * 3:ct * 3 + 1], t1buf[:, i, 0:2],
                        OP.mult, OP.add)
            Sv(conv_lrA)
            drip()
            stage()
            Sv(lambda w, cts=cts, wr=wr: [
                STT(w, t1buf[:, i, 0:wr], PS[i][:, 0:wr],
                    convws[:, ct * 3 + 2:ct * 3 + 3], t1buf[:, i, 0:wr],
                    OP.mult, OP.add) for i, ct in enumerate(cts)])
            drip()
            stage()
            Ss(lambda w, cts=cts: [ACT(w, hs_q[:, ct, :], t1buf[:, i, :],
                                       AF.Silu) for i, ct in enumerate(cts)])
            drip()
            for ct in cts:
                macx = macc if ct % 2 == 0 else maccB
                minx = mincc if ct % 2 == 0 else minccB
                hq_v.append(mk_hmm(ct, macx, OP.max))
                hq_v.append(mk_hmm(ct, minx, OP.min))
        stage()
        Sv(lambda w: CP(w, arE[:, 4096:4160], arE[:, 4160:4224]))
        drip()
        while hq_v or hq_g:
            stage()
            drip()

        if debug and q == 0:
            stage()

            def dbg5(w):
                for c in range(KH):
                    dma_out(w, dbg["dbg_hs"].rearrange(
                        "(k p) t -> k p t", p=128)[c], hs_q[:, c, :])
                w.wait(st, cnt["st"])
            Sy(dbg5)

        stage()
        Sv(lambda w: (TT(w, macc, macc, maccB, OP.max),
                      TT(w, mincc, mincc, minccB, OP.min)))
        stage()
        Sv(lambda w: TS(w, mincc, mincc, -1.0, None, OP.mult))
        stage()
        Sv(lambda w: TT(w, macc, macc, mincc, OP.max))
        stage()
        Ss(lambda w: ACT(w, xst[0][:, 0:512], macc[0:128, 0:512], AF.Copy))
        stage()
        St(lambda w: [TRN(w, PS[0][0:128, b * 128:(b + 1) * 128],
                          xst[0][:, b * 128:(b + 1) * 128])
                      for b in range(4)])
        stage()

        def hq_tm(w):
            for b in range(4):
                RMAX(w, stt[:, b:b + 1], PS[0][0:128, b * 128:(b + 1) * 128])
            TS(w, stt[:, 0:4], stt[:, 0:4], 1e-5, None, OP.max)
            RECIP(w, stt[:, 4:8], stt[:, 0:4])
            TS(w, stt[:, 4:8], stt[:, 4:8], 127.0, None, OP.mult)
            TS(w, stt[:, 8:12], stt[:, 0:4], m2col[:, 0:1],
               1.0 / 127.0, OP.mult, OP.mult)
        Sv(hq_tm)
        stage()

        def hq_rowify(w):
            for pi, cbase in [(5, 4), (6, 8)]:
                for b in range(4):
                    TRN(w, PS[pi][0:1, b * 128:b * 128 + 128],
                        stt[0:128, cbase + b:cbase + b + 1])
        St(hq_rowify)
        stage()
        Ss(lambda w: [ACT(w, row(3)[0:1, 0:TG], PS[5][0:1, 0:TG], AF.Copy),
                      ACT(w, row(4)[0:1, 0:TG], PS[6][0:1, 0:TG], AF.Copy)])
        stage()

        def hq_bcast(w):
            MM(w, PS[0][:, 0:TG], onesrow[0:1, :], row(3)[0:1, 0:TG],
               True, True)
            MM(w, PS[1][:, 0:TG], onesrow[0:1, :], row(4)[0:1, 0:TG],
               True, True)
        St(hq_bcast)
        stage()
        Ss(lambda w: ACT(w, csb_sb, PS[1][:, 0:TG], AF.Copy))
        pend_cent = [None]
        for cg in range(4):
            cts = list(range(cg * 8, cg * 8 + 8))
            stage()
            Sv(lambda w, cts=cts: [TT(w, u0buf[:, i, :], hs_q[:, ct, :],
                                      PS[0][:, 0:TG], OP.mult)
                                   for i, ct in enumerate(cts)])
            if pend_cent[0] is not None:
                Ss(pend_cent[0])
                pend_cent[0] = None
            stage()
            Ss(lambda w: [ACT(w, dbuf[:, i, :], u0buf[:, i, :], AF.Identity,
                              bias=c128[:, 0:1]) for i in range(8)])
            stage()

            Sv(lambda w: [TT(w, u0buf[:, i, :], u0buf[:, i, :], dbuf[:, i, :],
                             OP.subtract) for i in range(8)])
            stage()

            def hq_r(w, cts=cts):
                for i, ct in enumerate(cts):
                    STT(w, hs_q[:, ct, :], u0buf[:, i, :], -127.5,
                        dbuf[:, i, :], OP.is_gt, OP.add)
            Sv(hq_r)

            def hq_cent(w, cts=cts):
                for ct in cts:
                    ACT(w, hs_q[:, ct, :], hs_q[:, ct, :], AF.Identity,
                        bias=cm128[:, 0:1])
            pend_cent[0] = hq_cent

        w2h = [arD_bf[:, 0:4096].rearrange("p (k j) -> p k j", k=4),
               arD_bf[:, 4096:8192].rearrange("p (k j) -> p k j", k=4)]

        def ldw2(w, sc):
            dma_in(w, w2h[sc % 2],
                   w2t_dram.rearrange("(k p) j -> p k j", p=128)
                   [:, sc * 4:(sc + 1) * 4, :])
            dma_wait(w)
        stage()
        Sy(lambda w: ldw2(w, 0))
        Ss(pend_cent[0])
        pend_cent[0] = None
        for sc in range(8):
            stage()
            if sc + 1 < 8:
                Sy(lambda w, sc=sc: ldw2(w, sc + 1))

            def w2_mm(w, sc=sc):
                for jt in range(KD):
                    for i in range(4):
                        MM(w, PS[jt][:, 0:TG],
                           w2h[sc % 2][:, i, jt * 128:(jt + 1) * 128],
                           hs_q[:, sc * 4 + i, :], sc == 0 and i == 0,
                           sc == 7 and i == 3)
            St(w2_mm)
        stage()
        Sv(lambda w: [TT(w, out2_bf[:, jt, :], PS[jt][:, 0:TG], csb_sb,
                         OP.mult) for jt in range(KD)])

        if debug and q == 0:
            stage()

            def dbg6(w):
                for k in range(KD):
                    dma_out(w, dbg["dbg_out2"].rearrange(
                        "(k p) t -> k p t", p=128)[k], out2_bf[:, k, :])
                w.wait(st, cnt["st"])
            Sy(dbg6)

        stage()
        Ss(lambda w: [ACT(w, o2sq8[:, k, :], out2_bf[:, k, :], AF.Square)
                      for k in range(KD)])
        stage()

        def lnc_mm(w):
            for k in range(KD):
                MM(w, PS[0][0:1, 0:TG], wn_ones[:, k:k + 1], out2_bf[:, k, :],
                   k == 0, k == KD - 1)
            for k in range(KD):
                MM(w, PS[1][0:1, 0:TG], wn_ones[:, k:k + 1], o2sq8[:, k, :],
                   k == 0, k == KD - 1)
        St(lnc_mm)
        stage()
        Ss(lambda w: [ACT(w, row(0), PS[0][0:1, 0:TG], AF.Copy),
                      ACT(w, row(1), PS[1][0:1, 0:TG], AF.Copy)])
        stage()

        def lnc_m1(w):
            TS(w, row(0), row(0), DINV, None, OP.mult)
            TS(w, row(1), row(1), DINV, None, OP.mult)
            TT(w, row(2), row(0), row(0), OP.mult)
            TT(w, row(1), row(1), row(2), OP.subtract)
            TS(w, row(1), row(1), EPS, None, OP.add)
            RECIP(w, row(1), row(1))
        Sv(lnc_m1)
        stage()
        Ss(lambda w: ACT(w, row(1), row(1), AF.Sqrt))
        stage()

        def lnc_m2(w):
            TT(w, row(0), row(0), row(1), OP.mult)
            TS(w, row(0), row(0), -1.0, None, OP.mult)
        Sv(lnc_m2)
        stage()

        def lnc_bcast(w):
            MM(w, PS[0][:, 0:TG], onesrow[0:1, :], row(1), True, True)
            MM(w, PS[1][:, 0:TG], onesrow[0:1, :], row(0), True, True)
        St(lnc_bcast)
        stage()

        def lnc_xf(w, csl=csl):
            for k in range(KD):
                TT(w, t1buf[:, k, :], out2_bf[:, k, :], PS[0][:, 0:TG],
                   OP.mult)
        Sv(lnc_xf)
        stage()
        Sv(lambda w: [TT(w, t1buf[:, k, :], t1buf[:, k, :], PS[1][:, 0:TG],
                         OP.add) for k in range(KD)])
        stage()
        Sv(lambda w, csl=csl: [TT(w, xfbuf[:, k, :], t1buf[:, k, :],
                                  x2bf[:, k, csl], OP.add) for k in range(KD)])
        stage()

        def xf_store(w, csl=csl, q=q):
            for k in range(KD):
                dma_out(w, outT_k[k][:, csl], xfbuf[:, k, :])
        Sy(xf_store)

    stage()
    Sy(lambda w: w.wait(st, cnt["st"]))

    lp = contextlib.ExitStack()
    try:
        lp.enter_context(nc.allow_low_precision(
            "bf16 matmul operands / bf16 stream per stated tolerance"))
    except Exception:
        pass
    with nc.Block() as block:
        def mk(engname):
            def body(eng):
                w = Wrap(eng)
                for si, stg in enumerate(stages):
                    if si > 0:
                        eng.wait_ge(bar, NENG * si)
                    w.begin_stage()
                    for fn in stg[engname]:
                        fn(w)
                    if w.last is not None:
                        w.last.then_inc(bar, 1)
                    else:
                        eng.sem_inc(bar, 1)
            return body
        block.sync(mk("sync"))
        block.vector(mk("vector"))
        block.scalar(mk("scalar"))
        block.tensor(mk("tensor"))
        block.gpsimd(mk("gpsimd"))

    lp.close()
    ctx.close()
    return nc


_CACHE = {}


def _host_inputs(inputs):
    f = np.float32
    x = np.ascontiguousarray(inputs["L0_attn_x"], dtype=f)
    shared = {
        "wq": np.ascontiguousarray(inputs["L0_attn_Wq"], dtype=f),
        "wk": np.ascontiguousarray(inputs["L0_attn_Wk"], dtype=f),
        "wv": np.ascontiguousarray(inputs["L0_attn_Wv"], dtype=f),
        "wo": np.ascontiguousarray(inputs["L0_attn_Wo"], dtype=f),
        "wbeta": np.ascontiguousarray(inputs["L0_attn_Wbeta"], dtype=f),
        "bbeta": np.ascontiguousarray(
            inputs["L0_attn_b_beta"], dtype=f).reshape(NH, 1),
        "bo128": np.ascontiguousarray(
            inputs["L0_attn_b_o"], dtype=f).reshape(KD, 128).T.copy(),
        "g_row": np.ascontiguousarray(
            inputs["L0_attn_ln_g"], dtype=f).reshape(1, D),
        "lnb_row": np.ascontiguousarray(
            inputs["L0_attn_ln_b"], dtype=f).reshape(1, D),
        "g128": np.ascontiguousarray(
            inputs["L0_attn_ln_g"], dtype=f).reshape(KD, 128).T.copy(),
        "lnb128": np.ascontiguousarray(
            inputs["L0_attn_ln_b"], dtype=f).reshape(KD, 128).T.copy(),
        "convw": np.ascontiguousarray(
            inputs["L0_ffn_conv_w"], dtype=f).reshape(KH, 128, 3)
            .transpose(1, 0, 2).reshape(128, KH * 3).copy(),
        "convb": np.ascontiguousarray(
            inputs["L0_ffn_conv_b"], dtype=f).reshape(KH, 128).T.copy(),
        "w1T": np.ascontiguousarray(inputs["L0_ffn_w1"].T, dtype=f),
        "w2T": np.ascontiguousarray(inputs["L0_ffn_w2"].T, dtype=f),
    }
    import ml_dtypes
    e16 = np.zeros((NH, KD * 128), dtype=f)
    for h in range(NH):
        ct, half = h // 2, h % 2
        e16[h, ct * 128 + half * 64:ct * 128 + half * 64 + 64] = 1.0
    shared["e16f"] = e16
    shared["e16b"] = e16.astype(ml_dtypes.bfloat16)
    shared["eye128"] = np.eye(128, dtype=f)
    shared["onesrow"] = np.ones((1, 128), dtype=f)
    in_maps = []
    for b in range(8):
        m = dict(shared)
        m["xT"] = np.ascontiguousarray(x[b].T)
        in_maps.append(m)
    return in_maps


def kernel(**inputs):
    if "nc" not in _CACHE:
        _CACHE["nc"] = build_program(debug=False)
    nc = _CACHE["nc"]
    in_maps = _host_inputs(inputs)
    res = run_bass_kernel_spmd(nc, in_maps, core_ids=list(range(8)))
    out = np.stack([np.ascontiguousarray(r["outT"].T) for r in res.results],
                   axis=0).astype(np.float32)
    return out


# version 23 (final)
# speedup vs baseline: 1.0579x; 1.0579x over previous
import numpy as np
import contextlib
import concourse.bass as bass
import concourse.mybir as mybir
from concourse.bass_utils import run_bass_kernel_spmd

F32 = mybir.dt.float32
BF16 = mybir.dt.bfloat16
I32 = mybir.dt.int32
OP = mybir.AluOpType
AF = mybir.ActivationFunctionType
AX = mybir.AxisListType.X

S = 2048
D = 1024
NH, HD = 16, 64
DH = 4096
KD = 8
KH = 32
TG = 512
NTG = 4
EPS = 1.0e-5
ENGS = ("sync", "vector", "scalar", "tensor", "gpsimd")
NENG = len(ENGS)
DINV = 1.0 / D


_DTSZ = {F32: 4, BF16: 2, I32: 4}


def _regions(aps):
    regs = []
    for a in aps:
        if a is None or not hasattr(a, "tensor"):
            continue
        es = _DTSZ.get(a.dtype, 4)
        pairs = list(a.ap)
        pstride, pnum = pairs[0]
        row_b = int(pstride) * es
        off_b = int(a.offset) * es
        p0 = off_b // row_b if row_b > 0 else 0
        c0 = off_b - p0 * row_b
        span = 0
        for stn in pairs[1:]:
            span += abs(int(stn[0])) * (int(stn[1]) - 1)
        regs.append((a.tensor.name, p0, p0 + int(pnum), c0,
                     c0 + span * es + es))
    return regs


def _clash(regs1, regs2):
    for r1 in regs1:
        for r2 in regs2:
            if (r1[0] == r2[0] and r1[1] < r2[2] and r2[1] < r1[2]
                    and r1[3] < r2[4] and r2[3] < r1[4]):
                return True
    return False


class Wrap:
    def __init__(self, eng):
        self.e = eng
        self.written = []
        self.readset = []
        self.last = None

    def begin_stage(self):
        self.written = []
        self.readset = []
        self.last = None

    def do(self, outs, ins, fn, no_drain=False):
        oreg = _regions(outs)
        ireg = _regions(ins)
        if not no_drain and (_clash(oreg, self.written)
                             or _clash(oreg, self.readset)
                             or _clash(ireg, self.written)):
            self.e.drain()
            self.written = []
            self.readset = []
        inst = fn(self.e)
        self.written.extend(oreg)
        self.readset.extend(ireg)
        self.last = inst
        return inst

    def wait(self, sem, val):
        self.e.wait_ge(sem, val)
        self.last = None

    def raw(self, fn):
        inst = fn(self.e)
        self.last = None
        return inst


def build_program(debug=False):
    nc = bass.Bass("TRN2", target_bir_lowering=False)

    def din(name, shape, dt=F32):
        return nc.dram_tensor(name, shape, dt, kind="ExternalInput").ap()

    xT = din("xT", [D, S])
    wq_d = din("wq", [D, D])
    wk_d = din("wk", [D, D])
    wv_d = din("wv", [D, D])
    wo_d = din("wo", [D, D])
    wbeta_d = din("wbeta", [D, NH])
    bbeta_d = din("bbeta", [NH, 1])
    bo_d = din("bo128", [128, KD])
    grow_d = din("g_row", [1, D], BF16)
    lrow_d = din("lnb_row", [1, D], BF16)
    g128_d = din("g128", [128, KD])
    lnb128_d = din("lnb128", [128, KD])
    convw_d = din("convw", [128, KH * 3])
    convb_d = din("convb", [128, KH])
    w1T_d = din("w1T", [D, DH])
    w2T_d = din("w2T", [DH, D])
    e16f_d = din("e16f", [NH, KD * 128])
    e16b_d = din("e16b", [NH, KD * 128], BF16)
    eye_d = din("eye128", [128, 128])
    onesrow_d = din("onesrow", [1, 128])

    outT = nc.dram_tensor("outT", [D, S], F32, kind="ExternalOutput").ap()
    w1t_dram = nc.dram_tensor("w1t_scr", [D, DH], BF16, kind="Internal").ap()
    w2t_dram = nc.dram_tensor("w2t_scr", [DH, D], BF16, kind="Internal").ap()

    dbg = {}
    if debug:
        for nm, shape in [("dbg_xinq", [D, S]), ("dbg_oT", [D, S]),
                          ("dbg_y", [D, S]), ("dbg_x2", [D, S]),
                          ("dbg_xqv", [D, TG]), ("dbg_hs", [DH, TG]),
                          ("dbg_out2", [D, TG])]:
            dbg[nm] = nc.dram_tensor(nm, shape, BF16, kind="ExternalOutput").ap()
        dbg["dbg_stt"] = nc.dram_tensor("dbg_stt", [128, 32], F32,
                                        kind="ExternalOutput").ap()
        dbg["dbg_ps"] = nc.dram_tensor("dbg_ps", [128, 24], F32,
                                       kind="ExternalOutput").ap()
        dbg["dbg_q"] = nc.dram_tensor("dbg_q", [128, 2048], F32,
                                      kind="ExternalOutput").ap()
        dbg["dbg_rows"] = nc.dram_tensor("dbg_rows", [1, 10 * TG], F32,
                                         kind="ExternalOutput").ap()

    ctx = contextlib.ExitStack()

    def sbuf(name, shape, dt):
        return ctx.enter_context(nc.sbuf_tensor(name, shape, dt))

    x2bf_t = sbuf("x2bf", [128, KD * S], BF16)
    x2bf = x2bf_t[:].rearrange("p (k t) -> p k t", k=KD)

    arB_t = sbuf("arB", [128, KD * S], F32)
    arB_bf = arB_t[:].bitcast(BF16)
    xinq = arB_bf[:, 0:KD * S].rearrange("p (k t) -> p k t", k=KD)
    oTv = arB_bf[:, KD * S:2 * KD * S].rearrange("p (c t) -> p c t", c=KD)
    hs_q = arB_bf[:, 0:KH * TG].rearrange("p (c t) -> p c t", c=KH)
    w1g = [arB_bf[:, KD * S + i * 8192:KD * S + (i + 1) * 8192]
           .rearrange("p (k c) -> p k c", k=KD) for i in range(2)]

    arC_t = sbuf("arC", [128, 8192], F32)
    arC = arC_t[:]
    beta_b = arC[:, 0:S]
    vsqs_b = arC[:, S:2 * S]
    u_sc = arC[:, 2 * S:3 * S]
    g_sc = arC[:, 3 * S:4 * S]
    Yv = arC_t[:].bitcast(BF16)[:, 0:KD * S].rearrange("p (k t) -> p k t", k=KD)
    t1buf = arC[:, 0:8 * TG].rearrange("p (c t) -> p c t", c=8)
    dbuf = arC[:, 0:8 * TG].bitcast(I32).rearrange("p (c t) -> p c t", c=8)
    u0buf = arC[:, 8 * TG:16 * TG].rearrange("p (c t) -> p c t", c=8)
    xfbuf = arC[:, 8 * TG:16 * TG].rearrange("p (k t) -> p k t", k=KD)
    wpT = [arC[:, 4096 + i * 1024:4096 + (i + 1) * 1024].bitcast(BF16)
           for i in range(2)]

    arD_t = sbuf("arD", [128, 4096], F32)
    arD_bf = arD_t[:].bitcast(BF16)
    wk_bf = arD_bf.rearrange("p (k j) -> p k j", k=KD)
    wo_bf = wk_bf
    ysq = arD_bf[:, 0:KD * TG].rearrange("p (k t) -> p k t", k=KD)
    x2sq = ysq
    w2chunk = arD_bf.rearrange("p (k j) -> p k j", k=8)

    arE_t = sbuf("arE", [128, 4736], F32)
    mmx = sbuf("mmx", [128, 1024], BF16)
    arE = arE_t[:]
    xst = [arE[:, 0:2048], arE[:, 2048:4096]]
    pbuf = [arE[:, 1024:1536], arE[:, 1536:2048],
            arE[:, 3072:3584], arE[:, 3584:4096]]
    betaT = arE[0:NH, 0:S]
    vsum_bf = arE[0:NH, S:S + S // 2].bitcast(BF16)
    qsum_bf = arE[0:NH, S + S // 2:2 * S].bitcast(BF16)
    out2_bf = arE[:, 0:2048].bitcast(BF16).rearrange("p (k t) -> p k t", k=KD)
    xq_q = arE[:, 2048:4096].bitcast(BF16).rearrange("p (k t) -> p k t", k=KD)
    o2sq8 = xq_q
    macc = mmx[:, 0:512]
    maccB = mmx[:, 512:1024]
    mincc = arE[:, 1024:1536]
    minccB = arE[:, 1536:2048]
    halo = [arE[:, 4096 + i * 64:4096 + (i + 1) * 64]
            .rearrange("p (c w) -> p c w", c=KH) for i in range(2)]
    csb_sb = arE[:, 4224:4736]

    ROWS_t = sbuf("rows", [1, 10 * TG], F32)
    ROWB_t = sbuf("rowsb", [1, 8 * TG], BF16)

    def row(i):
        return ROWS_t[0:1, i * TG:(i + 1) * TG]

    def rowb(i):
        return ROWB_t[0:1, i * TG:(i + 1) * TG]

    e16f = sbuf("e16f_s", [NH, KD * 128], F32)
    e16b = sbuf("e16b_s", [NH, KD * 128], BF16)
    eye = sbuf("eye_s", [128, 128], F32)
    eye_b = sbuf("eye_b", [128, 128], BF16)
    onesrow = sbuf("onesr", [1, 128], F32)
    onesrow_b = sbuf("onesrb", [1, 128], BF16)
    g_row_b = sbuf("g_rowB", [1, D], BF16)
    lnb_row_b = sbuf("lnb_rowB", [1, D], BF16)
    onescol_f = sbuf("onescf", [128, 1], F32)
    onescol_b = sbuf("onescb", [128, 1], BF16)

    g128 = sbuf("g128S", [128, KD], F32)
    lnb128 = sbuf("lnb128S", [128, KD], F32)
    bo_sb = sbuf("bo_sb", [128, KD], F32)
    bbeta_sb = sbuf("bbeta_sb", [NH, 1], F32)
    convw_sb = sbuf("convw_s", [128, KH * 3], F32)
    convws = sbuf("convws", [128, KH * 3], F32)
    convb_sb = sbuf("convb_s", [128, KH], F32)
    wqs_f = sbuf("wqs_f", [128, KD * NH], F32)
    wqs_bf = sbuf("wqs_bf", [128, KD * NH], BF16)
    wvs_bf = sbuf("wvs_bf", [128, KD * NH], BF16)
    wbe_bf = sbuf("wbe_bf", [128, KD * NH], BF16)
    wn_ones = sbuf("wn_ones", [128, KD], BF16)
    wn_g = sbuf("wn_g", [128, KD], BF16)
    wn_g2 = sbuf("wn_g2", [128, KD], BF16)
    wn_gl = sbuf("wn_gl", [128, KD], BF16)
    tmp128 = sbuf("tmp128", [128, KD], F32)
    SCAL = sbuf("SCAL", [1, 16], F32)
    sw1col = sbuf("sw1col", [128, 2], F32)
    sw2col = sbuf("sw2col", [128, 2], F32)
    m1col = sbuf("m1col", [128, 1], F32)
    m2col = sbuf("m2col", [128, 1], F32)
    c128 = sbuf("c128", [128, 1], F32)
    cm128 = sbuf("cm128", [128, 1], F32)
    s127c = sbuf("s127c", [128, 1], F32)
    wrow = sbuf("wrow", [1, 132], F32)
    stt = sbuf("stt", [128, 32], F32)
    absred = sbuf("absred", [128, 16], F32)
    t0col = sbuf("t0col", [128, KD], BF16)
    t0u = sbuf("t0u", [128, KD], F32)
    t0d = sbuf("t0d", [128, KD], I32)

    PS = [ctx.enter_context(nc.psum_tensor(f"ps{i}", [128, TG], F32))
          for i in range(8)]

    bar = ctx.enter_context(nc.semaphore("bar"))
    ld = ctx.enter_context(nc.semaphore("ld"))
    st = ctx.enter_context(nc.semaphore("st"))
    stp = [ctx.enter_context(nc.semaphore("sta")),
           ctx.enter_context(nc.semaphore("stb"))]
    cnt = {"ld": 0, "st": 0, "sta": 0, "stb": 0}

    stages = []
    side = []
    cur = [stages]

    def to_side():
        cur[0] = side

    def to_main():
        cur[0] = stages

    def stage():
        cur[0].append({e: [] for e in ENGS})

    def Sy(fn):
        cur[0][-1]["sync"].append(fn)

    def Sv(fn):
        cur[0][-1]["vector"].append(fn)

    def Ss(fn):
        cur[0][-1]["scalar"].append(fn)

    def St(fn):
        cur[0][-1]["tensor"].append(fn)

    def Sg(fn):
        cur[0][-1]["gpsimd"].append(fn)

    def zip_side(anchor, limit):
        assert anchor + len(side) <= limit, (anchor, len(side), limit)
        for i, stg in enumerate(side):
            for e in ENGS:
                stages[anchor + i][e].extend(stg[e])
        side.clear()

    def splice_side(limit):
        stages[limit:limit] = list(side)
        side.clear()

    def dma_in(w, dst, src):
        cnt["ld"] += 16
        w.raw(lambda e: e.dma_start(out=dst, in_=src).then_inc(ld, 16))

    def dma_wait(w):
        w.wait(ld, cnt["ld"])

    def dma_out(w, dst, src):
        cnt["st"] += 16
        w.raw(lambda e: e.dma_start(out=dst, in_=src).then_inc(st, 16))

    def TT(w, out, a, b, op):
        w.do([out], [a, b], lambda e: e.tensor_tensor(out, a, b, op))

    def TS(w, out, a, s1, s2, op0, op1=OP.bypass):
        ins = [a] + [x for x in (s1, s2) if hasattr(x, "tensor")]
        w.do([out], ins, lambda e: e.tensor_scalar(out, a, s1, s2, op0, op1))

    def STT(w, out, in0, sc, in1, op0, op1):
        ins = [in0, in1] + ([sc] if hasattr(sc, "tensor") else [])
        w.do([out], ins,
             lambda e: e.scalar_tensor_tensor(out, in0, sc, in1, op0, op1))

    def CP(w, out, a):
        w.do([out], [a], lambda e: e.tensor_copy(out, a))

    def ACT(w, out, a, func, bias=0.0, scale=1.0):
        ins = [a] + [x for x in (bias, scale) if hasattr(x, "tensor")]
        w.do([out], ins,
             lambda e: e.activation(out, a, func, bias=bias, scale=scale))

    def RECIP(w, out, a):
        w.do([out], [a], lambda e: e.reciprocal(out, a))

    def MM(w, out, lhsT, rhs, start, stop):
        w.do([out], [lhsT, rhs],
             lambda e: e.matmul(out, lhsT, rhs, start=start, stop=stop),
             no_drain=not start)

    def TRN(w, out, a):
        w.do([out], [a, eye[:]], lambda e: e.transpose(out, a, eye[:]))

    def TRNB(w, out, a):
        w.do([out], [a, eye_b[:]], lambda e: e.transpose(out, a, eye_b[:]))

    def MEMS(w, out, v):
        w.do([out], [], lambda e: e.memset(out, v))

    def RMAX(w, out, a):
        w.do([out], [a], lambda e: e.reduce_max(out, a, axis=AX))

    def RMIN(w, out, a):
        w.do([out], [a], lambda e: e.tensor_reduce(out, a, AX, OP.min))

    def RSUM(w, out, a, absv=False):
        w.do([out], [a], lambda e: e.reduce_sum(
            out, a, axis=AX, apply_absolute_value=absv))

    xT_k = xT.rearrange("(k p) t -> k p t", p=128)
    w1T_k = w1T_d.rearrange("(k p) c -> k p c", p=128)
    w2T_k = w2T_d.rearrange("(k p) j -> k p j", p=128)
    w1s_k = w1t_dram.rearrange("(k p) c -> k p c", p=128)
    w2s_k = w2t_dram.rearrange("(k p) j -> k p j", p=128)
    outT_k = outT.rearrange("(k p) t -> k p t", p=128)
    attnw_k = {"wq": wq_d.rearrange("(k p) j -> k p j", p=128),
               "wk": wk_d.rearrange("(k p) j -> k p j", p=128),
               "wv": wv_d.rearrange("(k p) j -> k p j", p=128),
               "wo": wo_d.rearrange("(k p) j -> k p j", p=128)}
    wbeta_k = wbeta_d.rearrange("(k p) h -> k p h", p=128)

    stage()

    def ld_consts(w):
        for dst, src in [(e16f[:], e16f_d), (e16b[:], e16b_d), (eye[:], eye_d),
                         (onesrow[:], onesrow_d), (g_row_b[:], grow_d),
                         (lnb_row_b[:], lrow_d), (g128[:], g128_d),
                         (lnb128[:], lnb128_d), (bo_sb[:], bo_d),
                         (bbeta_sb[:], bbeta_d), (convw_sb[:], convw_d),
                         (convb_sb[:], convb_d)]:
            dma_in(w, dst, src)
        dma_wait(w)
    Sy(ld_consts)

    stage()

    def prep_consts(w):
        MEMS(w, stt[:], 0.0)
        MEMS(w, onescol_f[:], 1.0)
        MEMS(w, onescol_b[:], 1.0)
        MEMS(w, c128[:], 128.0)
        MEMS(w, cm128[:], -128.0)
        MEMS(w, s127c[:], 1.0 / 127.0)
        CP(w, eye_b[:], eye[:])
        CP(w, onesrow_b[:], onesrow[:])
        MEMS(w, wn_ones[:], 1.0)
        CP(w, wn_g[:], g128[:])
        TT(w, stt[:, 0:KD], g128[:], g128[:], OP.mult)
        CP(w, wn_g2[:], stt[:, 0:KD])
        TT(w, stt[:, KD:2 * KD], g128[:], lnb128[:], OP.mult)
        CP(w, wn_gl[:], stt[:, KD:2 * KD])
        TT(w, stt[:, 2 * KD:3 * KD], lnb128[:], lnb128[:], OP.mult)
        MEMS(w, macc[:], 0.0)
        MEMS(w, mincc[:], 0.0)
    Sv(prep_consts)

    stage()

    def sconst_mm(w):
        srcs = [g128[:], stt[:, 0:KD], lnb128[:], stt[:, KD:2 * KD],
                stt[:, 2 * KD:3 * KD]]
        for ci, src in enumerate(srcs):
            for k in range(KD):
                MM(w, PS[0][0:1, ci:ci + 1], src[:, k:k + 1],
                   onescol_f[:, 0:1], k == 0, k == KD - 1)
    St(sconst_mm)
    stage()
    Ss(lambda w: ACT(w, SCAL[0:1, 0:5], PS[0][0:1, 0:5], AF.Copy))

    xsq = arC[:, TG:2 * TG]
    stage()
    Sy(lambda w: (dma_in(w, xst[0], xT_k[0]), dma_wait(w)))
    for k in range(KD):
        for tg in range(NTG):
            stage()
            if tg == 0 and k + 1 < KD:
                Sy(lambda w, k=k: (dma_in(w, xst[(k + 1) % 2], xT_k[k + 1]),
                                   dma_wait(w)))
            Ss(lambda w, k=k, tg=tg: ACT(
                w, xsq, xst[k % 2][:, tg * TG:(tg + 1) * TG], AF.Square))
            stage()

            def ln1mm(w, k=k, tg=tg):
                MM(w, PS[tg][0:1, 0:TG], onescol_f[:, 0:1],
                   xst[k % 2][:, tg * TG:(tg + 1) * TG], k == 0, k == KD - 1)
                MM(w, PS[4 + tg][0:1, 0:TG], onescol_f[:, 0:1], xsq,
                   k == 0, k == KD - 1)
            St(ln1mm)

    stage()

    def ln1_evac(w):
        for tg in range(NTG):
            ACT(w, row(tg), PS[tg][0:1, 0:TG], AF.Copy)
            ACT(w, row(4 + tg), PS[4 + tg][0:1, 0:TG], AF.Copy)
    Ss(ln1_evac)
    stage()

    def ln1_math1(w):
        for tg in range(NTG):
            TS(w, row(tg), row(tg), DINV, None, OP.mult)
            TS(w, row(4 + tg), row(4 + tg), DINV, None, OP.mult)
            TT(w, row(8), row(tg), row(tg), OP.mult)
            TT(w, row(4 + tg), row(4 + tg), row(8), OP.subtract)
            TS(w, row(4 + tg), row(4 + tg), EPS, None, OP.add)
            RECIP(w, row(4 + tg), row(4 + tg))
    Sv(ln1_math1)
    stage()
    Ss(lambda w: [ACT(w, row(4 + tg), row(4 + tg), AF.Sqrt)
                  for tg in range(NTG)])
    stage()

    def ln1_math2(w):
        for tg in range(NTG):
            TT(w, row(tg), row(tg), row(4 + tg), OP.mult)
            TS(w, row(tg), row(tg), -1.0, None, OP.mult)
    Sv(ln1_math2)

    stage()
    Ss(lambda w: [ACT(w, rowb(i), row(i), AF.Copy) for i in range(8)])
    stage()

    def ln1_bcast(w):
        for tg in range(NTG):
            MM(w, PS[tg][:, 0:TG], onesrow_b[0:1, :], rowb(4 + tg),
               True, True)
            MM(w, PS[4 + tg][:, 0:TG], onesrow_b[0:1, :], rowb(tg),
               True, True)
    St(ln1_bcast)
    stage()
    Sy(lambda w: (dma_in(w, xst[0], xT_k[0]), dma_wait(w)))

    def ln1_aff1(w, k):
        sl = (k % 2) * 4
        for tg in range(NTG):
            TT(w, t1buf[:, sl + tg, :], xst[k % 2][:, tg * TG:(tg + 1) * TG],
               PS[tg][:, 0:TG], OP.mult)

    def ln1_aff2(w, k):
        sl = (k % 2) * 4
        for tg in range(NTG):
            TT(w, xinq[:, k, tg * TG:(tg + 1) * TG], t1buf[:, sl + tg, :],
               PS[4 + tg][:, 0:TG], OP.add)

    for k in range(KD):
        stage()
        if k + 1 < KD:
            Sy(lambda w, k=k: (dma_in(w, xst[(k + 1) % 2], xT_k[k + 1]),
                               dma_wait(w)))
        Sv(lambda w, k=k: (ln1_aff1(w, k),
                           ln1_aff2(w, k - 1) if k > 0 else None))
    stage()
    Sv(lambda w: ln1_aff2(w, KD - 1))

    def wprep(name, per_chunk, sched=None, lag=None):
        srck = attnw_k[name] if name != "wbeta" else wbeta_k
        ncols = 1024 if name != "wbeta" else NH
        sched = sched if sched is not None else Sv
        stage()
        Sy(lambda w: (dma_in(w, xst[0][:, 0:ncols], srck[0]), dma_wait(w)))
        for k in range(KD):
            stage()
            if k + 1 < KD:
                Sy(lambda w, k=k: (dma_in(w, xst[(k + 1) % 2][:, 0:ncols],
                                          srck[k + 1]), dma_wait(w)))
            sched(lambda w, k=k: per_chunk(w, k, xst[k % 2]))
            if lag is not None and k > 0:
                Ss(lambda w, k=k: lag(w, k - 1))
        if lag is not None:
            stage()
            Ss(lambda w: lag(w, KD - 1))

    def wqs_chunk(w, k, src):
        RSUM(w, wqs_f[:, k * NH:(k + 1) * NH],
             src[:, 0:1024].rearrange("p (h j) -> p h j", j=HD))

    def wqs_lag(w, k):
        ACT(w, wqs_bf[:, k * NH:(k + 1) * NH],
            wqs_f[:, k * NH:(k + 1) * NH], AF.Copy)

    def wvs_chunk(w, k, src):
        RSUM(w, wqs_f[:, k * NH:(k + 1) * NH],
             src[:, 0:1024].rearrange("p (h j) -> p h j", j=HD))

    def wvs_lag(w, k):
        ACT(w, wvs_bf[:, k * NH:(k + 1) * NH],
            wqs_f[:, k * NH:(k + 1) * NH], AF.Copy)

    def wbe_chunk(w, k, src):
        CP(w, wbe_bf[:, k * NH:(k + 1) * NH], src[:, 0:NH])

    def wk_chunk(w, k, src):
        ACT(w, wk_bf[:, k, :], src[:, 0:1024], AF.Copy)

    wprep("wq", wqs_chunk, lag=wqs_lag)
    wprep("wv", wvs_chunk, lag=wvs_lag)
    wprep("wbeta", wbe_chunk)
    wprep("wk", wk_chunk, sched=Ss)

    for tg in range(NTG):
        stage()

        def thin_mm(w, tg=tg):
            for ci, lhs in enumerate([wqs_bf, wvs_bf, wbe_bf]):
                for k in range(KD):
                    MM(w, PS[ci][0:NH, 0:TG], lhs[:, k * NH:(k + 1) * NH],
                       xinq[:, k, tg * TG:(tg + 1) * TG], k == 0, k == KD - 1)
        St(thin_mm)
        stage()

        def thin_evac(w, tg=tg):
            sl = slice(tg * TG, (tg + 1) * TG)
            ACT(w, qsum_bf[:, sl], PS[0][0:NH, 0:TG], AF.Copy)
            ACT(w, vsum_bf[:, sl], PS[1][0:NH, 0:TG], AF.Copy)
            ACT(w, betaT[:, sl], PS[2][0:NH, 0:TG], AF.Sigmoid,
                bias=bbeta_sb[:, 0:1])
        Ss(thin_evac)

    for ct in range(KD):
        ef = e16f[:, ct * 128:(ct + 1) * 128]
        eb = e16b[:, ct * 128:(ct + 1) * 128]
        stage()
        St(lambda w, ef=ef: [MM(w, PS[tg][:, 0:TG], ef,
                                betaT[:, tg * TG:(tg + 1) * TG], True, True)
                             for tg in range(NTG)])
        stage()
        Ss(lambda w: [ACT(w, beta_b[:, tg * TG:(tg + 1) * TG],
                          PS[tg][:, 0:TG], AF.Copy) for tg in range(NTG)])
        stage()

        def scan_t2(w, eb=eb, ct=ct):
            for tg in range(NTG):
                MM(w, PS[tg][:, 0:TG], eb, vsum_bf[:, tg * TG:(tg + 1) * TG],
                   True, True)
            for tg in range(NTG):
                for k in range(KD):
                    MM(w, PS[4 + tg][:, 0:TG],
                       wk_bf[:, k, ct * 128:(ct + 1) * 128],
                       xinq[:, k, tg * TG:(tg + 1) * TG], k == 0, k == KD - 1)
        St(scan_t2)
        stage()
        Ss(lambda w: [ACT(w, vsqs_b[:, tg * TG:(tg + 1) * TG],
                          PS[tg][:, 0:TG], AF.Copy) for tg in range(NTG)])
        stage()
        Sv(lambda w: [TT(w, u_sc[:, tg * TG:(tg + 1) * TG], PS[4 + tg][:, 0:TG],
                         vsqs_b[:, tg * TG:(tg + 1) * TG], OP.mult)
                      for tg in range(NTG)])
        stage()
        Sv(lambda w: w.do([g_sc], [beta_b, u_sc],
                          lambda e: e.tensor_tensor_scan(
                              g_sc, beta_b, u_sc, 0.0, OP.mult, OP.add)))
        St(lambda w, eb=eb: [MM(w, PS[tg][:, 0:TG], eb,
                                qsum_bf[:, tg * TG:(tg + 1) * TG], True, True)
                             for tg in range(NTG)])
        stage()
        Sv(lambda w, ct=ct: [TT(w, oTv[:, ct, tg * TG:(tg + 1) * TG],
                                g_sc[:, tg * TG:(tg + 1) * TG],
                                PS[tg][:, 0:TG], OP.mult)
                             for tg in range(NTG)])

    if debug:
        stage()

        def dbg1(w):
            for c in range(KD):
                dma_out(w, dbg["dbg_oT"].rearrange("(k p) t -> k p t", p=128)[c],
                        oTv[:, c, :])
                dma_out(w, dbg["dbg_xinq"].rearrange(
                    "(k p) t -> k p t", p=128)[c], xinq[:, c, :])
            w.wait(st, cnt["st"])
        Sy(dbg1)

    stage()
    Sy(lambda w: (dma_in(w, xst[0][:, 0:1024], attnw_k["wo"][0]), dma_wait(w)))
    for k in range(KD):
        stage()
        if k + 1 < KD:
            Sy(lambda w, k=k: (dma_in(w, xst[(k + 1) % 2][:, 0:1024],
                                      attnw_k["wo"][k + 1]), dma_wait(w)))
        Ss(lambda w, k=k: ACT(w, wo_bf[:, k, :], xst[k % 2][:, 0:1024],
                              AF.Copy))

    for tg in range(NTG):
        stage()

        def wo_mm(w, tg=tg):
            for jt in range(KD):
                for c in range(KD):
                    MM(w, PS[jt][:, 0:TG], wo_bf[:, c, jt * 128:(jt + 1) * 128],
                       oTv[:, c, tg * TG:(tg + 1) * TG], c == 0, c == KD - 1)
        St(wo_mm)
        stage()
        Sv(lambda w, tg=tg: [STT(w, Yv[:, jt, tg * TG:(tg + 1) * TG],
                                 PS[jt][:, 0:TG], bo_sb[:, jt:jt + 1],
                                 xinq[:, jt, tg * TG:(tg + 1) * TG],
                                 OP.add, OP.add) for jt in range(KD)])

    if debug:
        stage()

        def dbg2(w):
            for k in range(KD):
                dma_out(w, dbg["dbg_y"].rearrange("(k p) t -> k p t", p=128)[k],
                        Yv[:, k, :])
            w.wait(st, cnt["st"])
        Sy(dbg2)

    lnab_mark = len(stages)
    for tg in range(NTG):
        stage()
        Ss(lambda w, tg=tg: [ACT(w, ysq[:, k, :],
                                 Yv[:, k, tg * TG:(tg + 1) * TG], AF.Square)
                             for k in range(KD)])
        stage()

        def lnab_mm(w, tg=tg):
            cfg = [(wn_ones, "y"), (wn_g, "y"), (wn_g2, "y"), (wn_gl, "y"),
                   (wn_ones, "q"), (wn_g2, "q")]
            for ci, (lhs, rv) in enumerate(cfg):
                for k in range(KD):
                    rhs = (Yv[:, k, tg * TG:(tg + 1) * TG] if rv == "y"
                           else ysq[:, k, :])
                    MM(w, PS[ci][0:1, 0:TG], lhs[:, k:k + 1], rhs,
                       k == 0, k == KD - 1)
        St(lnab_mm)
        stage()
        Ss(lambda w: [ACT(w, row(i), PS[i][0:1, 0:TG], AF.Copy)
                      for i in range(6)])
        stage()

        def lnab_m1(w):
            TS(w, row(0), row(0), DINV, None, OP.mult)
            TS(w, row(4), row(4), DINV, None, OP.mult)
            TT(w, row(6), row(0), row(0), OP.mult)
            TT(w, row(4), row(4), row(6), OP.subtract)
            TS(w, row(4), row(4), EPS, None, OP.add)
            RECIP(w, row(4), row(4))
        Sv(lnab_m1)
        stage()
        Ss(lambda w: ACT(w, row(4), row(4), AF.Sqrt))
        stage()

        def lnab_m2(w):
            TS(w, row(6), row(0), SCAL[0:1, 0:1], None, OP.mult)
            TT(w, row(1), row(1), row(6), OP.subtract)
            TT(w, row(1), row(1), row(4), OP.mult)
            TS(w, row(1), row(1), SCAL[0:1, 2:3], None, OP.add)
            TS(w, row(1), row(1), DINV, None, OP.mult)
            TS(w, row(6), row(0), SCAL[0:1, 1:2], None, OP.mult)
            TT(w, row(6), row(6), row(0), OP.mult)
            TS(w, row(8), row(2), -2.0, None, OP.mult)
            TT(w, row(8), row(8), row(0), OP.mult)
            TT(w, row(5), row(5), row(6), OP.add)
            TT(w, row(5), row(5), row(8), OP.add)
            TT(w, row(8), row(4), row(4), OP.mult)
            TT(w, row(5), row(5), row(8), OP.mult)
            TS(w, row(8), row(0), SCAL[0:1, 3:4], None, OP.mult)
            TT(w, row(8), row(3), row(8), OP.subtract)
            TT(w, row(8), row(8), row(4), OP.mult)
            TS(w, row(8), row(8), 2.0, None, OP.mult)
            TT(w, row(5), row(5), row(8), OP.add)
            TS(w, row(5), row(5), SCAL[0:1, 4:5], None, OP.add)
            TS(w, row(5), row(5), DINV, None, OP.mult)
            TT(w, row(8), row(1), row(1), OP.mult)
            TT(w, row(5), row(5), row(8), OP.subtract)
            TS(w, row(5), row(5), EPS, None, OP.add)
            RECIP(w, row(5), row(5))
        Sv(lnab_m2)
        stage()
        Ss(lambda w: ACT(w, row(7), row(5), AF.Sqrt))
        stage()

        def lnab_m3(w):
            TT(w, row(6), row(4), row(7), OP.mult)
            TT(w, row(8), row(0), row(6), OP.mult)
            TS(w, row(8), row(8), -1.0, None, OP.mult)
            TT(w, row(9), row(1), row(7), OP.mult)
            TS(w, row(9), row(9), -1.0, None, OP.mult)
        Sv(lnab_m3)
        stage()
        Ss(lambda w: [ACT(w, rowb(i - 2), row(i), AF.Copy)
                      for i in (6, 7, 8, 9)])
        for half in range(2):
            ks = list(range(half * 4, half * 4 + 4))
            stage()

            def x_half_load(w, ks=ks, tg=tg):
                for i, k in enumerate(ks):
                    dma_in(w, xst[i % 2][:, (i // 2) * TG:(i // 2 + 1) * TG],
                           xT_k[k][:, tg * TG:(tg + 1) * TG])
                dma_wait(w)
            Sy(x_half_load)

            def lnab_bc(w, ks=ks):
                for i, k in enumerate(ks):
                    MM(w, PS[2 * i][:, 0:TG],
                       g_row_b[0:1, k * 128:(k + 1) * 128], rowb(4),
                       True, True)
                    MM(w, PS[2 * i + 1][:, 0:TG],
                       lnb_row_b[0:1, k * 128:(k + 1) * 128], rowb(5),
                       True, False)
                    MM(w, PS[2 * i + 1][:, 0:TG],
                       g_row_b[0:1, k * 128:(k + 1) * 128], rowb(6),
                       False, False)
                    MM(w, PS[2 * i + 1][:, 0:TG], onesrow_b[0:1, :], rowb(7),
                       False, True)
            St(lnab_bc)
            stage()

            def lnab_x2a(w, ks=ks, tg=tg):
                for i, k in enumerate(ks):
                    TT(w, pbuf[i], Yv[:, k, tg * TG:(tg + 1) * TG],
                       PS[2 * i][:, 0:TG], OP.mult)
            Sv(lnab_x2a)
            stage()
            Sv(lambda w, ks=ks: [TT(w, pbuf[i], pbuf[i], PS[2 * i + 1][:, 0:TG],
                                    OP.add) for i in range(4)])
            stage()

            def lnab_x2c(w, ks=ks, tg=tg):
                for i, k in enumerate(ks):
                    xc = xst[i % 2][:, (i // 2) * TG:(i // 2 + 1) * TG]
                    TT(w, x2bf[:, k, tg * TG:(tg + 1) * TG], pbuf[i], xc,
                       OP.add)
            Sv(lnab_x2c)

    if debug:
        stage()

        def dbg3(w):
            for k in range(KD):
                dma_out(w, dbg["dbg_x2"].rearrange("(k p) t -> k p t", p=128)[k],
                        x2bf[:, k, :])
            w.wait(st, cnt["st"])
        Sy(dbg3)

    PKEY = ("sta", "stb")
    bld = {"sta": 0, "stb": 0}

    def dma_out_p(w, dst, src, p):
        cnt[PKEY[p]] += 16
        w.raw(lambda e: e.dma_start(out=dst, in_=src).then_inc(stp[p], 16))

    x2f = x2bf_t[:].bitcast(F32)
    x2braw = x2bf_t[:]
    wst = [x2f[:, 0:2048], x2f[:, 2048:4096]]
    wpA2 = x2f[:, 4096:5120]
    wpB2 = x2f[:, 5120:6144]
    wpT2 = [x2braw[:, 12288:13312], x2braw[:, 13312:14336]]

    to_side()

    def wsum2(chunks, mcol, swcol, dstview=None):
        dv = dstview if dstview is not None else (lambda buf: buf)
        nch = len(chunks)
        stage()
        Sy(lambda w: (dma_in(w, dv(wst[0]), chunks[0]), dma_wait(w)))
        for ch in range(nch):
            stage()
            if ch + 1 < nch:
                Sy(lambda w, ch=ch: (dma_in(w, dv(wst[(ch + 1) % 2]),
                                            chunks[ch + 1]), dma_wait(w)))
            Sv(lambda w, ch=ch: RSUM(w, absred[:, ch:ch + 1], wst[ch % 2],
                                     absv=True))
        stage()
        Sv(lambda w, nch=nch: RSUM(w, t0u[:, 0:1], absred[:, 0:nch]))
        stage()
        Sy(lambda w: (dma_in(w, wrow[0:1, 0:128], t0u[:, 0:1]),
                      dma_wait(w)))
        stage()

        def wmean_v(w):
            RSUM(w, wrow[0:1, 128:129], wrow[0:1, 0:128])
            TS(w, wrow[0:1, 129:130], wrow[0:1, 128:129], 1.0 / (D * DH),
               None, OP.mult)
            TS(w, wrow[0:1, 129:130], wrow[0:1, 129:130], 1e-5, None, OP.max)
            RECIP(w, wrow[0:1, 130:131], wrow[0:1, 129:130])
            TS(w, wrow[0:1, 0:128], onesrow[0:1, :], wrow[0:1, 129:130],
               None, OP.mult)
        Sv(wmean_v)
        stage()
        Sy(lambda w: (dma_in(w, mcol[:, 0:1], wrow[0:1, 0:128]),
                      dma_wait(w)))
        stage()
        Sv(lambda w: TS(w, wrow[0:1, 0:128], onesrow[0:1, :],
                        wrow[0:1, 130:131], None, OP.mult))
        stage()
        Sy(lambda w: (dma_in(w, swcol[:, 0:1], wrow[0:1, 0:128]),
                      dma_wait(w)))
        stage()
        Sv(lambda w: TS(w, swcol[:, 1:2], swcol[:, 0:1], -1.0, None, OP.mult))

    def tern2(chunks, dsts, swcol):
        nch = len(chunks)
        stage()
        Sy(lambda w: (dma_in(w, wst[0][:, 0:1024], chunks[0]), dma_wait(w)))
        for ch in range(nch + 1):
            stage()
            if ch < nch:
                if ch + 1 < nch:
                    Sy(lambda w, ch=ch: (
                        dma_in(w, wst[(ch + 1) % 2][:, 0:1024],
                               chunks[ch + 1]), dma_wait(w)))
                tgt = bld[PKEY[ch % 2]]

                def tern_g(w, ch=ch, tgt=tgt):
                    if ch >= 2:
                        w.wait(stp[ch % 2], tgt)
                    TS(w, wpA2, wst[ch % 2][:, 0:1024], swcol[:, 0:1], 0.5,
                       OP.mult, OP.is_gt)
                    TS(w, wpB2, wst[ch % 2][:, 0:1024], swcol[:, 1:2], 0.5,
                       OP.mult, OP.is_gt)
                    TT(w, wpT2[ch % 2], wpA2, wpB2, OP.subtract)
                Sv(tern_g)
            if ch > 0:
                bld[PKEY[(ch - 1) % 2]] += 16
                Sy(lambda w, ch=ch: dma_out_p(
                    w, dsts[ch - 1], wpT2[(ch - 1) % 2], (ch - 1) % 2))
        stage()
        tgt_a, tgt_b = bld["sta"], bld["stb"]
        Sy(lambda w: (w.wait(stp[0], tgt_a), w.wait(stp[1], tgt_b)))

    w1srcs = [w1T_k[kk][:, hh * 2048:(hh + 1) * 2048]
              for kk in range(KD) for hh in range(2)]
    w2srcs = [w2T_d.rearrange("(a b p) j -> a p b j", b=2, p=128)[i]
              for i in range(16)]
    wsum2(w1srcs, m1col, sw1col)
    wsum2([s for s in w2srcs], m2col, sw2col,
          dstview=lambda buf: buf.rearrange("p (b j) -> p b j", b=2))
    stage()
    Sv(lambda w: TS(w, convws[:], convw_sb[:], m1col[:, 0:1], None, OP.mult))
    to_main()
    zip_side(1, lnab_mark)
    to_side()
    w1tsrc = [w1T_k[kk][:, hh * 1024:(hh + 1) * 1024]
              for kk in range(KD) for hh in range(4)]
    w1tdst = [w1s_k[kk][:, hh * 1024:(hh + 1) * 1024]
              for kk in range(KD) for hh in range(4)]
    tern2(w1tsrc, w1tdst, sw1col)
    tern2([w2T_k[kk] for kk in range(KH)],
          [w2s_k[kk] for kk in range(KH)], sw2col)
    to_main()
    splice_side(lnab_mark)

    tok0 = x2bf[:, :, 0:1]
    stage()
    Ss(lambda w: [ACT(w, ysq[:, k, 0:1], x2bf[:, k, 0:1], AF.Square)
                  for k in range(KD)])
    stage()

    def t0_mm(w):
        for k in range(KD):
            MM(w, PS[0][0:1, 0:1], wn_ones[:, k:k + 1], x2bf[:, k, 0:1],
               k == 0, k == KD - 1)
        for k in range(KD):
            MM(w, PS[1][0:1, 0:1], wn_ones[:, k:k + 1], ysq[:, k, 0:1],
               k == 0, k == KD - 1)
    St(t0_mm)
    stage()

    def t0_maxmin(w):
        CP(w, stt[:, 8:9], x2bf[:, 0, 0:1])
        CP(w, stt[:, 9:10], x2bf[:, 0, 0:1])
        for k in range(1, KD):
            TT(w, stt[:, 8:9], stt[:, 8:9], x2bf[:, k, 0:1], OP.max)
            TT(w, stt[:, 9:10], stt[:, 9:10], x2bf[:, k, 0:1], OP.min)
    Sv(t0_maxmin)
    stage()

    def t0_trn(w):
        TRN(w, PS[2][0:1, 0:128], stt[:, 8:9])
        TRN(w, PS[3][0:1, 0:128], stt[:, 9:10])
    St(t0_trn)
    stage()

    def t0_red(w):
        RMAX(w, row(6)[0:1, 0:1], PS[2][0:1, 0:128])
        RMIN(w, row(6)[0:1, 1:2], PS[3][0:1, 0:128])
    Sv(t0_red)
    stage()
    Ss(lambda w: ACT(w, row(6)[0:1, 2:4], PS[0][0:1, 0:1], AF.Copy)
       if False else [ACT(w, row(6)[0:1, 2:3], PS[0][0:1, 0:1], AF.Copy),
                      ACT(w, row(6)[0:1, 3:4], PS[1][0:1, 0:1], AF.Copy)])
    stage()

    def t0_math1(w):
        r = row(6)
        TS(w, r[0:1, 2:3], r[0:1, 2:3], DINV, None, OP.mult)
        TS(w, r[0:1, 3:4], r[0:1, 3:4], DINV, None, OP.mult)
        TT(w, r[0:1, 4:5], r[0:1, 2:3], r[0:1, 2:3], OP.mult)
        TT(w, r[0:1, 3:4], r[0:1, 3:4], r[0:1, 4:5], OP.subtract)
        TS(w, r[0:1, 3:4], r[0:1, 3:4], EPS, None, OP.add)
        RECIP(w, r[0:1, 3:4], r[0:1, 3:4])
    Sv(t0_math1)
    stage()
    Ss(lambda w: ACT(w, row(6)[0:1, 3:4], row(6)[0:1, 3:4], AF.Sqrt))
    stage()

    def t0_math2(w):
        r = row(6)
        TT(w, r[0:1, 0:1], r[0:1, 0:1], r[0:1, 2:3], OP.subtract)
    Sv(t0_math2)
    stage()

    def t0_math3(w):
        r = row(6)
        TT(w, r[0:1, 1:2], r[0:1, 2:3], r[0:1, 1:2], OP.subtract)
        TT(w, r[0:1, 0:1], r[0:1, 0:1], r[0:1, 1:2], OP.max)
        TT(w, r[0:1, 0:1], r[0:1, 0:1], r[0:1, 3:4], OP.mult)
        TS(w, r[0:1, 0:1], r[0:1, 0:1], 1e-5, None, OP.max)
        TS(w, r[0:1, 4:5], r[0:1, 0:1], 1.0 / 127.0, None, OP.mult)
        RECIP(w, r[0:1, 1:2], r[0:1, 0:1])
        TS(w, r[0:1, 1:2], r[0:1, 1:2], 127.0, None, OP.mult)
        TT(w, r[0:1, 0:1], r[0:1, 3:4], r[0:1, 1:2], OP.mult)
        TT(w, r[0:1, 1:2], r[0:1, 2:3], r[0:1, 0:1], OP.mult)
        TS(w, r[0:1, 1:2], r[0:1, 1:2], -1.0, None, OP.mult)
        TS(w, r[0:1, 1:2], r[0:1, 1:2], 128.0, None, OP.add)
    Sv(t0_math3)
    stage()
    St(lambda w: [MM(w, PS[4][:, i:i + 1], onesrow[0:1, :],
                     row(6)[0:1, i:i + 1], True, True) for i in range(3)])
    stage()
    Ss(lambda w: ACT(w, stt[:, 12:15], PS[4][:, 0:3], AF.Copy))
    stage()

    def t0_quant(w):
        TS(w, t0u[:], tok0.rearrange("p k x -> p (k x)"), stt[:, 12:13],
           stt[:, 13:14], OP.mult, OP.add)
        CP(w, t0d[:], t0u[:])
    Sv(t0_quant)
    stage()
    Sv(lambda w: (TT(w, t0u[:], t0u[:], t0d[:], OP.subtract),
                  STT(w, t0u[:], t0u[:], 0.5, t0d[:], OP.is_gt, OP.add)))
    stage()
    St(lambda w: MM(w, PS[4][:, 4:5], onesrow[0:1, :], row(6)[0:1, 4:5],
                    True, True))
    stage()
    Ss(lambda w: ACT(w, stt[:, 14:15], PS[4][:, 4:5], AF.Copy))
    stage()
    Sv(lambda w: TS(w, t0col[:], t0u[:], 128.0, stt[:, 14:15],
                    OP.subtract, OP.mult))

    for q in range(NTG):
        rs_ = q * TG + 1
        wr = TG if q < 3 else TG - 1
        cs_ = q * TG
        rsl = slice(rs_, rs_ + wr)
        csl = slice(cs_, cs_ + TG)

        stage()
        Sy(lambda w: w.wait(st, cnt["st"]))

        def mm3v(w, ka, kb, rsl=rsl, wr=wr):
            for acc, k in ((macc, ka), (maccB, kb)):
                if k < 2:
                    CP(w, acc[0:128, 0:wr], x2bf[:, k, rsl])
                else:
                    TT(w, acc[0:128, 0:wr], acc[0:128, 0:wr],
                       x2bf[:, k, rsl], OP.max)

        def mm3g(w, ka, kb, rsl=rsl, wr=wr):
            for acc, k in ((mincc, ka), (minccB, kb)):
                TT(w, acc[0:128, 0:wr], acc[0:128, 0:wr],
                   x2bf[:, k, rsl], OP.min)

        stage()
        Ss(lambda w, rsl=rsl, wr=wr: [ACT(w, x2sq[:, k, 0:wr],
                                          x2bf[:, k, rsl], AF.Square)
                                      for k in range(KD)])
        Sv(lambda w, rsl=rsl, wr=wr: (mm3v(w, 0, 1),
                                      CP(w, mincc[0:128, 0:wr],
                                         x2bf[:, 0, rsl]),
                                      CP(w, minccB[0:128, 0:wr],
                                         x2bf[:, 1, rsl])))
        stage()

        def ln3_mm(w, rsl=rsl, wr=wr):
            for k in range(KD):
                MM(w, PS[0][0:1, 0:wr], wn_ones[:, k:k + 1], x2bf[:, k, rsl],
                   k == 0, k == KD - 1)
            for k in range(KD):
                MM(w, PS[1][0:1, 0:wr], wn_ones[:, k:k + 1], x2sq[:, k, 0:wr],
                   k == 0, k == KD - 1)
        St(ln3_mm)
        Sv(lambda w: (mm3v(w, 2, 3), mm3g(w, 2, 3)))
        stage()
        Sv(lambda w: (mm3v(w, 4, 5), mm3g(w, 4, 5)))
        stage()
        Ss(lambda w, wr=wr: [ACT(w, row(0)[0:1, 0:wr], PS[0][0:1, 0:wr],
                                 AF.Copy),
                             ACT(w, row(1)[0:1, 0:wr], PS[1][0:1, 0:wr],
                                 AF.Copy)])
        Sv(lambda w: (mm3v(w, 6, 7), mm3g(w, 6, 7)))
        stage()
        Sv(lambda w, wr=wr: (TT(w, macc[0:128, 0:wr], macc[0:128, 0:wr],
                                maccB[0:128, 0:wr], OP.max),
                             TT(w, mincc[0:128, 0:wr], mincc[0:128, 0:wr],
                                minccB[0:128, 0:wr], OP.min)))
        stage()

        def ln3_m1(w, wr=wr):
            TS(w, row(0)[0:1, 0:wr], row(0)[0:1, 0:wr], DINV, None, OP.mult)
            TS(w, row(1)[0:1, 0:wr], row(1)[0:1, 0:wr], DINV, None, OP.mult)
            TT(w, row(2)[0:1, 0:wr], row(0)[0:1, 0:wr], row(0)[0:1, 0:wr],
               OP.mult)
            TT(w, row(1)[0:1, 0:wr], row(1)[0:1, 0:wr], row(2)[0:1, 0:wr],
               OP.subtract)
            TS(w, row(1)[0:1, 0:wr], row(1)[0:1, 0:wr], EPS, None, OP.add)
            RECIP(w, row(1)[0:1, 0:wr], row(1)[0:1, 0:wr])
        Sv(ln3_m1)
        stage()
        Ss(lambda w, wr=wr: [ACT(w, row(1)[0:1, 0:wr], row(1)[0:1, 0:wr],
                                 AF.Sqrt),
                             ACT(w, xst[0][:, 0:wr], macc[0:128, 0:wr],
                                 AF.Copy),
                             ACT(w, xst[0][:, 512:512 + wr],
                                 mincc[0:128, 0:wr], AF.Copy)])
        stage()

        def ln3_trn(w, wr=wr):
            nb = (wr + 127) // 128
            for b in range(nb):
                bw = min(128, wr - b * 128)
                TRN(w, PS[2][0:bw, b * 128:b * 128 + 128],
                    xst[0][:, b * 128:b * 128 + bw])
                TRN(w, PS[3][0:bw, b * 128:b * 128 + 128],
                    xst[0][:, 512 + b * 128:512 + b * 128 + bw])
                MM(w, PS[4][0:bw, b:b + 1],
                   row(0)[0:1, b * 128:b * 128 + bw], onesrow[0:1, 0:1],
                   True, True)
                MM(w, PS[4][0:bw, 4 + b:5 + b],
                   row(1)[0:1, b * 128:b * 128 + bw], onesrow[0:1, 0:1],
                   True, True)
        St(ln3_trn)
        stage()

        def ln3_tm(w, wr=wr):
            nb = (wr + 127) // 128
            for b in range(nb):
                bw = min(128, wr - b * 128)
                RMAX(w, stt[0:bw, b:b + 1],
                     PS[2][0:bw, b * 128:b * 128 + 128])
                RMIN(w, stt[0:bw, 4 + b:5 + b],
                     PS[3][0:bw, b * 128:b * 128 + 128])
                CP(w, stt[0:bw, 8 + b:9 + b], PS[4][0:bw, b:b + 1])
                CP(w, stt[0:bw, 12 + b:13 + b], PS[4][0:bw, 4 + b:5 + b])
        Sv(ln3_tm)
        stage()

        def ln3_tmmath(w, wr=wr):
            nb = (wr + 127) // 128
            c = stt
            TT(w, c[:, 0:nb], c[:, 0:nb], c[:, 8:8 + nb], OP.subtract)
            TT(w, c[:, 4:4 + nb], c[:, 8:8 + nb], c[:, 4:4 + nb], OP.subtract)
            TT(w, c[:, 0:nb], c[:, 0:nb], c[:, 4:4 + nb], OP.max)
            TT(w, c[:, 0:nb], c[:, 0:nb], c[:, 12:12 + nb], OP.mult)
            TS(w, c[:, 0:nb], c[:, 0:nb], 1e-5, None, OP.max)
            TS(w, c[:, 16:16 + nb], c[:, 0:nb], 1.0 / 127.0, None, OP.mult)
            RECIP(w, c[:, 4:4 + nb], c[:, 0:nb])
            TS(w, c[:, 4:4 + nb], c[:, 4:4 + nb], 127.0, None, OP.mult)
            TT(w, c[:, 20:20 + nb], c[:, 12:12 + nb], c[:, 4:4 + nb],
               OP.mult)
            TT(w, c[:, 24:24 + nb], c[:, 8:8 + nb], c[:, 20:20 + nb], OP.mult)
            TS(w, c[:, 24:24 + nb], c[:, 24:24 + nb], -1.0, None, OP.mult)
            TS(w, c[:, 24:24 + nb], c[:, 24:24 + nb], 128.0, None, OP.add)
        Sv(ln3_tmmath)
        stage()

        def ln3_rowify(w, wr=wr):
            nb = (wr + 127) // 128
            for pi, cbase in [(5, 20), (6, 24), (7, 16)]:
                for b in range(nb):
                    TRN(w, PS[pi][0:1, b * 128:b * 128 + 128],
                        stt[0:128, cbase + b:cbase + b + 1])
        St(ln3_rowify)
        stage()
        Ss(lambda w, wr=wr: [ACT(w, row(3)[0:1, 0:wr], PS[5][0:1, 0:wr],
                                 AF.Copy),
                             ACT(w, row(4)[0:1, 0:wr], PS[6][0:1, 0:wr],
                                 AF.Copy),
                             ACT(w, row(5)[0:1, 0:wr], PS[7][0:1, 0:wr],
                                 AF.Copy)])
        stage()

        def ln3_bcast(w, wr=wr):
            MM(w, PS[0][:, 0:wr], onesrow[0:1, :], row(3)[0:1, 0:wr],
               True, True)
            MM(w, PS[1][:, 0:wr], onesrow[0:1, :], row(4)[0:1, 0:wr],
               True, True)
            MM(w, PS[2][:, 0:wr], onesrow[0:1, :], row(5)[0:1, 0:wr],
               True, True)
        St(ln3_bcast)
        if debug and q == 0:
            stage()

            def dbgps_fn(w):
                ACT(w, csb_sb[:, 0:8], PS[0][:, 504:512], AF.Copy)
                ACT(w, csb_sb[:, 8:16], PS[1][:, 504:512], AF.Copy)
                ACT(w, csb_sb[:, 16:24], PS[2][:, 504:512], AF.Copy)
            Ss(dbgps_fn)
            stage()
            Sy(lambda w: (dma_out(w, dbg["dbg_ps"], csb_sb[:, 0:24]),
                          w.wait(st, cnt["st"])))
        stage()

        def qx_u(w, rsl=rsl, wr=wr):
            for k in range(KD):
                TT(w, t1buf[:, k, 0:wr], x2bf[:, k, rsl], PS[0][:, 0:wr],
                   OP.mult)
        Sv(qx_u)
        stage()
        Sv(lambda w, wr=wr: [TT(w, u0buf[:, k, 0:wr], t1buf[:, k, 0:wr],
                                PS[1][:, 0:wr], OP.add) for k in range(KD)])
        if debug and q == 0:
            Sv(lambda w: CP(w, arB_t[:][:, 8192:8704], u0buf[:, 0, :]))
        stage()
        Ss(lambda w, wr=wr: [ACT(w, dbuf[:, k, 0:wr], u0buf[:, k, 0:wr],
                                 AF.Identity) for k in range(KD)])
        stage()
        if debug and q == 0:
            Sv(lambda w: CP(w, arB_t[:][:, 8704:9216], dbuf[:, 0, :]))
            stage()

        def qx_r(w, wr=wr):
            for k in range(KD):
                TT(w, u0buf[:, k, 0:wr], u0buf[:, k, 0:wr], dbuf[:, k, 0:wr],
                   OP.subtract)
        Sv(qx_r)
        stage()
        Sv(lambda w, wr=wr: [STT(w, u0buf[:, k, 0:wr], u0buf[:, k, 0:wr], 0.5,
                                 dbuf[:, k, 0:wr], OP.is_gt, OP.add)
                             for k in range(KD)])
        stage()
        Sv(lambda w, wr=wr: [STT(w, xq_q[:, k, 0:wr], u0buf[:, k, 0:wr],
                                 -128.0, PS[2][:, 0:wr], OP.add, OP.mult)
                             for k in range(KD)])
        if debug and q == 0:
            Sv(lambda w: (CP(w, arB_t[:][:, 9216:9728], u0buf[:, 0, :]),
                          CP(w, arB_t[:][:, 9728:10240], xq_q[:, 0, :])))

        if debug and q == 0:
            stage()

            def dbg4(w):
                for k in range(KD):
                    dma_out(w, dbg["dbg_xqv"].rearrange(
                        "(k p) t -> k p t", p=128)[k][:, 0:wr],
                        xq_q[:, k, 0:wr])
                dma_out(w, dbg["dbg_stt"], stt[:])
                dma_out(w, dbg["dbg_q"], arB_t[:][:, 8192:10240])
                dma_out(w, dbg["dbg_rows"], ROWS_t[0:1, :])
                w.wait(st, cnt["st"])
            Sy(dbg4)

        def ldw1g(w, bi, cg):
            dst = arB_bf[:, KD * S + bi * 8192:KD * S + (bi + 1) * 8192] \
                .rearrange("p (k c) -> p k c", k=KD)
            src = w1t_dram.rearrange("(k p) c -> p k c", p=128) \
                [:, :, cg * 1024:(cg + 1) * 1024]
            dma_in(w, dst, src)
            dma_wait(w)
        stage()
        Sy(lambda w: ldw1g(w, 0, 0))
        if q == 0:
            stage()
            Sv(lambda w: MEMS(w, arE[:, 4096:4160], 0.0))

        hq_v, hq_g = [], []

        def mk_hmm(ct, acc, op):
            first = ct < 2

            def f(w, ct=ct, acc=acc, op=op, first=first):
                if first:
                    CP(w, acc, hs_q[:, ct, :])
                else:
                    TT(w, acc, acc, hs_q[:, ct, :], op)
            return f

        def drip():
            for _ in range(4):
                if hq_v:
                    Sv(hq_v.pop(0))

        for cg in range(4):
            cts = list(range(cg * 8, cg * 8 + 8))
            if q == 0:
                stage()

                def strip_mm(w, cts=cts, cg=cg):
                    for i, ct in enumerate(cts):
                        for k in range(KD):
                            MM(w, PS[7][:, i:i + 1],
                               w1g[cg % 2][:, k, (ct % 8) * 128:
                                           (ct % 8) * 128 + 128],
                               t0col[:, k:k + 1], k == 0, k == KD - 1)
                St(strip_mm)
                drip()
                stage()
                Ss(lambda w, cts=cts: [ACT(w, halo[0][:, ct, 1:2],
                                           PS[7][:, i:i + 1], AF.Copy)
                                       for i, ct in enumerate(cts)])
                drip()
            stage()

            def conv_mm(w, cts=cts, cg=cg, rsl=rsl, wr=wr):
                for i, ct in enumerate(cts):
                    for k in range(KD):
                        MM(w, PS[i][:, 0:wr],
                           w1g[cg % 2][:, k, (ct % 8) * 128:(ct % 8) * 128 + 128],
                           xq_q[:, k, 0:wr], k == 0, k == KD - 1)
            St(conv_mm)
            if cg + 1 < 4:
                Sy(lambda w, cg=cg: ldw1g(w, (cg + 1) % 2, cg + 1))
            drip()
            stage()

            def conv_center(w, cts=cts, wr=wr):
                for i, ct in enumerate(cts):
                    ACT(w, halo[1][:, ct, 0:2], PS[i][:, wr - 2:wr], AF.Copy)
                    ACT(w, t1buf[:, i, 1:TG], PS[i][:, 0:TG - 1], AF.Identity,
                        bias=convb_sb[:, ct:ct + 1],
                        scale=convws[:, ct * 3 + 1:ct * 3 + 2])
                    ACT(w, t1buf[:, i, 0:1], halo[0][:, ct, 1:2], AF.Identity,
                        bias=convb_sb[:, ct:ct + 1],
                        scale=convws[:, ct * 3 + 1:ct * 3 + 2])
            Ss(conv_center)
            drip()
            stage()

            def conv_lrA(w, cts=cts, wr=wr):
                for i, ct in enumerate(cts):
                    STT(w, t1buf[:, i, 2:TG], PS[i][:, 0:TG - 2],
                        convws[:, ct * 3:ct * 3 + 1], t1buf[:, i, 2:TG],
                        OP.mult, OP.add)
                    STT(w, t1buf[:, i, 0:2], halo[0][:, ct, 0:2],
                        convws[:, ct * 3:ct * 3 + 1], t1buf[:, i, 0:2],
                        OP.mult, OP.add)
            Sv(conv_lrA)
            drip()
            stage()
            Sv(lambda w, cts=cts, wr=wr: [
                STT(w, t1buf[:, i, 0:wr], PS[i][:, 0:wr],
                    convws[:, ct * 3 + 2:ct * 3 + 3], t1buf[:, i, 0:wr],
                    OP.mult, OP.add) for i, ct in enumerate(cts)])
            drip()
            stage()
            Ss(lambda w, cts=cts: [ACT(w, hs_q[:, ct, :], t1buf[:, i, :],
                                       AF.Silu) for i, ct in enumerate(cts)])
            drip()
            for ct in cts:
                macx = macc if ct % 2 == 0 else maccB
                minx = mincc if ct % 2 == 0 else minccB
                hq_v.append(mk_hmm(ct, macx, OP.max))
                hq_v.append(mk_hmm(ct, minx, OP.min))
        stage()
        Sv(lambda w: CP(w, arE[:, 4096:4160], arE[:, 4160:4224]))
        drip()
        while hq_v or hq_g:
            stage()
            drip()

        if debug and q == 0:
            stage()

            def dbg5(w):
                for c in range(KH):
                    dma_out(w, dbg["dbg_hs"].rearrange(
                        "(k p) t -> k p t", p=128)[c], hs_q[:, c, :])
                w.wait(st, cnt["st"])
            Sy(dbg5)

        stage()
        Sv(lambda w: (TT(w, macc, macc, maccB, OP.max),
                      TT(w, mincc, mincc, minccB, OP.min)))
        stage()
        Sv(lambda w: TS(w, mincc, mincc, -1.0, None, OP.mult))
        stage()
        Sv(lambda w: TT(w, macc, macc, mincc, OP.max))
        stage()
        Ss(lambda w: ACT(w, xst[0][:, 0:512], macc[0:128, 0:512], AF.Copy))
        stage()
        St(lambda w: [TRN(w, PS[0][0:128, b * 128:(b + 1) * 128],
                          xst[0][:, b * 128:(b + 1) * 128])
                      for b in range(4)])
        stage()

        def hq_tm(w):
            for b in range(4):
                RMAX(w, stt[:, b:b + 1], PS[0][0:128, b * 128:(b + 1) * 128])
            TS(w, stt[:, 0:4], stt[:, 0:4], 1e-5, None, OP.max)
            RECIP(w, stt[:, 4:8], stt[:, 0:4])
            TS(w, stt[:, 4:8], stt[:, 4:8], 127.0, None, OP.mult)
            TS(w, stt[:, 8:12], stt[:, 0:4], m2col[:, 0:1],
               1.0 / 127.0, OP.mult, OP.mult)
        Sv(hq_tm)
        stage()

        def hq_rowify(w):
            for pi, cbase in [(5, 4), (6, 8)]:
                for b in range(4):
                    TRN(w, PS[pi][0:1, b * 128:b * 128 + 128],
                        stt[0:128, cbase + b:cbase + b + 1])
        St(hq_rowify)
        stage()
        Ss(lambda w: [ACT(w, row(3)[0:1, 0:TG], PS[5][0:1, 0:TG], AF.Copy),
                      ACT(w, row(4)[0:1, 0:TG], PS[6][0:1, 0:TG], AF.Copy)])
        stage()

        def hq_bcast(w):
            MM(w, PS[0][:, 0:TG], onesrow[0:1, :], row(3)[0:1, 0:TG],
               True, True)
            MM(w, PS[1][:, 0:TG], onesrow[0:1, :], row(4)[0:1, 0:TG],
               True, True)
        St(hq_bcast)
        stage()
        Ss(lambda w: ACT(w, csb_sb, PS[1][:, 0:TG], AF.Copy))
        pend_cent = [None]
        for cg in range(4):
            cts = list(range(cg * 8, cg * 8 + 8))
            stage()
            Sv(lambda w, cts=cts: [TT(w, u0buf[:, i, :], hs_q[:, ct, :],
                                      PS[0][:, 0:TG], OP.mult)
                                   for i, ct in enumerate(cts)])
            if pend_cent[0] is not None:
                Ss(pend_cent[0])
                pend_cent[0] = None
            stage()
            Ss(lambda w: [ACT(w, dbuf[:, i, :], u0buf[:, i, :], AF.Identity,
                              bias=c128[:, 0:1]) for i in range(8)])
            stage()

            Sv(lambda w: [TT(w, u0buf[:, i, :], u0buf[:, i, :], dbuf[:, i, :],
                             OP.subtract) for i in range(8)])
            stage()

            def hq_r(w, cts=cts):
                for i, ct in enumerate(cts):
                    STT(w, hs_q[:, ct, :], u0buf[:, i, :], -127.5,
                        dbuf[:, i, :], OP.is_gt, OP.add)
            Sv(hq_r)

            def hq_cent(w, cts=cts):
                for ct in cts:
                    ACT(w, hs_q[:, ct, :], hs_q[:, ct, :], AF.Identity,
                        bias=cm128[:, 0:1])
            pend_cent[0] = hq_cent

        w2h = [arD_bf[:, 0:4096].rearrange("p (k j) -> p k j", k=4),
               arD_bf[:, 4096:8192].rearrange("p (k j) -> p k j", k=4)]

        def ldw2(w, sc):
            dma_in(w, w2h[sc % 2],
                   w2t_dram.rearrange("(k p) j -> p k j", p=128)
                   [:, sc * 4:(sc + 1) * 4, :])
            dma_wait(w)
        stage()
        Sy(lambda w: ldw2(w, 0))
        Ss(pend_cent[0])
        pend_cent[0] = None
        for sc in range(8):
            stage()
            if sc + 1 < 8:
                Sy(lambda w, sc=sc: ldw2(w, sc + 1))

            def w2_mm(w, sc=sc):
                for jt in range(KD):
                    for i in range(4):
                        MM(w, PS[jt][:, 0:TG],
                           w2h[sc % 2][:, i, jt * 128:(jt + 1) * 128],
                           hs_q[:, sc * 4 + i, :], sc == 0 and i == 0,
                           sc == 7 and i == 3)
            St(w2_mm)
        stage()
        Sv(lambda w: [TT(w, out2_bf[:, jt, :], PS[jt][:, 0:TG], csb_sb,
                         OP.mult) for jt in range(KD)])

        if debug and q == 0:
            stage()

            def dbg6(w):
                for k in range(KD):
                    dma_out(w, dbg["dbg_out2"].rearrange(
                        "(k p) t -> k p t", p=128)[k], out2_bf[:, k, :])
                w.wait(st, cnt["st"])
            Sy(dbg6)

        stage()
        Ss(lambda w: [ACT(w, o2sq8[:, k, :], out2_bf[:, k, :], AF.Square)
                      for k in range(KD)])
        stage()

        def lnc_mm(w):
            for k in range(KD):
                MM(w, PS[0][0:1, 0:TG], wn_ones[:, k:k + 1], out2_bf[:, k, :],
                   k == 0, k == KD - 1)
            for k in range(KD):
                MM(w, PS[1][0:1, 0:TG], wn_ones[:, k:k + 1], o2sq8[:, k, :],
                   k == 0, k == KD - 1)
        St(lnc_mm)
        stage()
        Ss(lambda w: [ACT(w, row(0), PS[0][0:1, 0:TG], AF.Copy),
                      ACT(w, row(1), PS[1][0:1, 0:TG], AF.Copy)])
        stage()

        def lnc_m1(w):
            TS(w, row(0), row(0), DINV, None, OP.mult)
            TS(w, row(1), row(1), DINV, None, OP.mult)
            TT(w, row(2), row(0), row(0), OP.mult)
            TT(w, row(1), row(1), row(2), OP.subtract)
            TS(w, row(1), row(1), EPS, None, OP.add)
            RECIP(w, row(1), row(1))
        Sv(lnc_m1)
        stage()
        Ss(lambda w: ACT(w, row(1), row(1), AF.Sqrt))
        stage()

        def lnc_m2(w):
            TT(w, row(0), row(0), row(1), OP.mult)
            TS(w, row(0), row(0), -1.0, None, OP.mult)
        Sv(lnc_m2)
        stage()
        Ss(lambda w: [ACT(w, rowb(i), row(i), AF.Copy) for i in (0, 1)])
        stage()

        def lnc_bcast(w):
            MM(w, PS[0][:, 0:TG], onesrow_b[0:1, :], rowb(1), True, True)
            MM(w, PS[1][:, 0:TG], onesrow_b[0:1, :], rowb(0), True, True)
        St(lnc_bcast)
        stage()

        def lnc_xf(w, csl=csl):
            for k in range(KD):
                TT(w, t1buf[:, k, :], out2_bf[:, k, :], PS[0][:, 0:TG],
                   OP.mult)
        Sv(lnc_xf)
        stage()
        Sv(lambda w: [TT(w, t1buf[:, k, :], t1buf[:, k, :], PS[1][:, 0:TG],
                         OP.add) for k in range(KD)])
        stage()
        Sv(lambda w, csl=csl: [TT(w, xfbuf[:, k, :], t1buf[:, k, :],
                                  x2bf[:, k, csl], OP.add) for k in range(KD)])
        stage()

        def xf_store(w, csl=csl, q=q):
            for k in range(KD):
                dma_out(w, outT_k[k][:, csl], xfbuf[:, k, :])
        Sy(xf_store)

    stage()
    Sy(lambda w: w.wait(st, cnt["st"]))

    lp = contextlib.ExitStack()
    try:
        lp.enter_context(nc.allow_low_precision(
            "bf16 matmul operands / bf16 stream per stated tolerance"))
    except Exception:
        pass
    with nc.Block() as block:
        def mk(engname):
            def body(eng):
                w = Wrap(eng)
                for si, stg in enumerate(stages):
                    if si > 0:
                        eng.wait_ge(bar, NENG * si)
                    w.begin_stage()
                    for fn in stg[engname]:
                        fn(w)
                    if w.last is not None:
                        w.last.then_inc(bar, 1)
                    else:
                        eng.sem_inc(bar, 1)
            return body
        block.sync(mk("sync"))
        block.vector(mk("vector"))
        block.scalar(mk("scalar"))
        block.tensor(mk("tensor"))
        block.gpsimd(mk("gpsimd"))

    lp.close()
    ctx.close()
    return nc


_CACHE = {}


def _host_inputs(inputs):
    f = np.float32
    x = np.ascontiguousarray(inputs["L0_attn_x"], dtype=f)
    shared = {
        "wq": np.ascontiguousarray(inputs["L0_attn_Wq"], dtype=f),
        "wk": np.ascontiguousarray(inputs["L0_attn_Wk"], dtype=f),
        "wv": np.ascontiguousarray(inputs["L0_attn_Wv"], dtype=f),
        "wo": np.ascontiguousarray(inputs["L0_attn_Wo"], dtype=f),
        "wbeta": np.ascontiguousarray(inputs["L0_attn_Wbeta"], dtype=f),
        "bbeta": np.ascontiguousarray(
            inputs["L0_attn_b_beta"], dtype=f).reshape(NH, 1),
        "bo128": np.ascontiguousarray(
            inputs["L0_attn_b_o"], dtype=f).reshape(KD, 128).T.copy(),
        "g_row": None,
        "lnb_row": None,
        "g128": np.ascontiguousarray(
            inputs["L0_attn_ln_g"], dtype=f).reshape(KD, 128).T.copy(),
        "lnb128": np.ascontiguousarray(
            inputs["L0_attn_ln_b"], dtype=f).reshape(KD, 128).T.copy(),
        "convw": np.ascontiguousarray(
            inputs["L0_ffn_conv_w"], dtype=f).reshape(KH, 128, 3)
            .transpose(1, 0, 2).reshape(128, KH * 3).copy(),
        "convb": np.ascontiguousarray(
            inputs["L0_ffn_conv_b"], dtype=f).reshape(KH, 128).T.copy(),
        "w1T": np.ascontiguousarray(inputs["L0_ffn_w1"].T, dtype=f),
        "w2T": np.ascontiguousarray(inputs["L0_ffn_w2"].T, dtype=f),
    }
    import ml_dtypes
    shared["g_row"] = np.ascontiguousarray(
        inputs["L0_attn_ln_g"], dtype=f).reshape(1, D).astype(
        ml_dtypes.bfloat16)
    shared["lnb_row"] = np.ascontiguousarray(
        inputs["L0_attn_ln_b"], dtype=f).reshape(1, D).astype(
        ml_dtypes.bfloat16)
    e16 = np.zeros((NH, KD * 128), dtype=f)
    for h in range(NH):
        ct, half = h // 2, h % 2
        e16[h, ct * 128 + half * 64:ct * 128 + half * 64 + 64] = 1.0
    shared["e16f"] = e16
    shared["e16b"] = e16.astype(ml_dtypes.bfloat16)
    shared["eye128"] = np.eye(128, dtype=f)
    shared["onesrow"] = np.ones((1, 128), dtype=f)
    in_maps = []
    for b in range(8):
        m = dict(shared)
        m["xT"] = np.ascontiguousarray(x[b].T)
        in_maps.append(m)
    return in_maps


def kernel(**inputs):
    if "nc" not in _CACHE:
        _CACHE["nc"] = build_program(debug=False)
    nc = _CACHE["nc"]
    in_maps = _host_inputs(inputs)
    res = run_bass_kernel_spmd(nc, in_maps, core_ids=list(range(8)))
    out = np.stack([np.ascontiguousarray(r["outT"].T) for r in res.results],
                   axis=0).astype(np.float32)
    return out
